# version 01
# baseline (speedup 1.0000x reference)
import numpy as np
import ml_dtypes
from contextlib import ExitStack
import concourse.bass as bass
import concourse.mybir as mybir
from concourse.bass_utils import run_bass_kernel_spmd

F32 = mybir.dt.float32
BF16 = mybir.dt.bfloat16
I32 = mybir.dt.int32
AF = mybir.ActivationFunctionType
ALU = mybir.AluOpType

B = 4
S = 4096
D = 4096
FF = 11008
T = 2048
NCD = D // 128
NTF = FF // 128
KVL = 512
KVF = 576
QL = 1536
NH = 32
NOPE = 128
ROPE = 64
QH = NOPE + ROPE
VH = 128
SCALE = float(1.0 / np.sqrt(QH))
EPS = 1e-5

ENGS = ("sync", "gpsimd", "vector", "scalar", "tensor")


class Ctr:

    def __init__(self, handle):
        self.h = handle
        self.v = 0

    def inc(self, n=1):
        self.v += n
        return self.v


class Plan:
    def __init__(self, nc, es):
        self.nc = nc
        self.es = es
        self.ops = {e: [] for e in ENGS}
        self.dsems = []
        self._uid = [0]

    def dsem(self, eng, name):
        self._uid[0] += 1
        c = Ctr(self.es.enter_context(self.nc.semaphore(f"{name}_{self._uid[0]}")))
        self.dsems.append((eng, c))
        return c

    def emit(self, eng, fn):
        self.ops[eng].append(fn)

    def wait(self, eng, ctr, val):
        if val <= 0:
            return
        self.emit(eng, lambda e, h=ctr.h, v=val: e.wait_ge(h, v))

    def run(self, block):
        for eng in ENGS:
            ops = self.ops[eng]

            def body(e, ops=ops):
                for f in ops:
                    f(e)

            getattr(block, eng)(body)


def dma(plan, eng, dsem, out, in_, **kw):
    v = dsem.inc(16)
    plan.emit(eng, lambda e, o=out, i=in_, h=dsem.h, kw=kw:
              e.dma_start(out=o, in_=i, **kw).then_inc(h, 16))
    return v


def op(plan, eng, ctr, fn):
    v = ctr.inc(1)
    plan.emit(eng, lambda e, fn=fn, h=ctr.h: fn(e).then_inc(h, 1))
    return v


def opq(plan, eng, fn):
    plan.emit(eng, fn)


def barrier(plan, bar, compute=()):
    for eng, c in plan.dsems:
        plan.wait(eng, c, c.v)
    for eng, c in compute:
        plan.wait(eng, c, c.v)
    v = bar.inc(len(ENGS))
    for eng in ENGS:
        plan.emit(eng, lambda e, h=bar.h: e.nop().then_inc(h, 1))
    for eng in ENGS:
        plan.emit(eng, lambda e, h=bar.h, v=v: e.wait_ge(h, v))
    return v



def emit_l2_attention(nc, plan, es, sbL, sbR, pb, sV, sS, sT, sB, CMP, id_s, ones_s,
                      bqcT_s, bqupT_s, bkvupT_s, boT_s,
                      x1T, x2T, kv16, wqcTh, wqupTh, wkvupTh, wo16, xn2h,
                      emit_stats, emit_xn, emit_xn_hbm, mkdsem, dsConst, v_const,
                      dsCast, cast_done, dma, barrier):
    ln2_es = ExitStack()
    mb2 = ln2_es.enter_context(sbR("mb2", [128, T], F32))
    invb2 = ln2_es.enter_context(sbR("invb2", [128, T], F32))
    emit_stats(x1T, mb2, invb2)
    barrier(plan, sB, compute=CMP)

    att_es = ExitStack()
    qc_s = att_es.enter_context(sbL("qc_s", [128, 12 * T], BF16))

    emit_xn_hbm(x1T, mb2, invb2, xn2h)
    ln2_es.close()
    barrier(plan, sB, compute=CMP)

    dsWq = [mkdsem("sync", f"dsWq{k}") for k in range(2)]
    dsXn2 = [mkdsem("sync", f"dsXn2{k}") for k in range(2)]
    with sbL("wqf0", [128, NCD * 128], BF16) as wqf0, \
         sbL("wqf1", [128, NCD * 128], BF16) as wqf1, \
         sbL("xnc0", [128, T], BF16) as xnc0, \
         sbL("xnc1", [128, T], BF16) as xnc1:
        wqf = [wqf0, wqf1]
        xnc = [xnc0, xnc1]
        use_wq = [0, 0]
        use_xn = [0, 0]
        qc_ev = [0, 0]
        for ft in range(12):
            sl = ft % 2
            bk0 = 4 * sl
            plan.wait("sync", sT, use_wq[sl])
            src = wqcTh.rearrange("p (c f) -> p c f", f=QL)[:, :, ft * 128:(ft + 1) * 128]
            wv = dma(plan, "sync", dsWq[sl],
                     wqf[sl][:].rearrange("p (c j) -> p c j", j=128), src)
            plan.wait("tensor", dsWq[sl], wv)
            plan.wait("tensor", sV, qc_ev[sl])
            for c in range(NCD):
                xsl = c % 2
                plan.wait("sync", sT, use_xn[xsl])
                xlv = dma(plan, "sync", dsXn2[xsl], xnc[xsl][:],
                          xn2h[c * 128:(c + 1) * 128, :])
                plan.wait("tensor", dsXn2[xsl], xlv)
                for nt in range(4):
                    fn = lambda e, c=c, nt=nt, sl=sl, bk0=bk0, xsl=xsl: e.matmul(
                        pb[bk0 + nt][:, :], wqf[sl][:, c * 128:(c + 1) * 128],
                        xnc[xsl][:, nt * 512:(nt + 1) * 512],
                        start=(c == 0), stop=(c == NCD - 1))
                    if nt == 3:
                        use_xn[xsl] = mmv = op(plan, "tensor", sT, fn)
                    else:
                        opq(plan, "tensor", fn)
            use_wq[sl] = mmv
            plan.wait("vector", sT, mmv)
            for nt in range(4):
                fn = lambda e, ft=ft, nt=nt, bk0=bk0: e.tensor_scalar_add(
                    qc_s[:, ft * T + nt * 512:ft * T + (nt + 1) * 512],
                    pb[bk0 + nt][:, :], bqcT_s[:, ft:ft + 1])
                if nt == 3:
                    qc_ev[sl] = op(plan, "vector", sV, fn)
                else:
                    opq(plan, "vector", fn)
    barrier(plan, sB, compute=CMP)

    kvn_s = [att_es.enter_context(sbL(f"kvn{l}", [128, S], BF16)) for l in range(4)]
    krop_s = att_es.enter_context(sbL("krop", [65, S], BF16))
    krop2_s = att_es.enter_context(sbL("krop2", [64, S], BF16))
    woT_s = att_es.enter_context(sbL("woT", [128, D], BF16))
    o_sum = att_es.enter_context(sbL("o_sum", [128, 16 * VH], F32))
    osT = att_es.enter_context(sbL("osT", [128, 16 * VH], BF16))
    boc_s = att_es.enter_context(sbL("boc", [128, 32], F32))
    sqbup_s = att_es.enter_context(sbL("sqbup", [128, 64], F32))
    kmx_s = att_es.enter_context(sbL("kmx", [1, 16], F32))
    qn2r_s = att_es.enter_context(sbL("qn2r", [1, T], F32))
    srow_s = att_es.enter_context(sbL("srow", [1, T], F32))
    bvs_s = att_es.enter_context(sbL("bvs", [128, 1], F32))
    bvs16_s = att_es.enter_context(sbL("bvs16", [128, 1], BF16))

    dsKvL = mkdsem("sync", "dsKvL")
    plan.wait("sync", dsCast["a"], cast_done["a"])
    for l in range(4):
        dma(plan, "sync", dsKvL, kvn_s[l][:], kv16[l * 128:(l + 1) * 128, :])
    kv_lv = dma(plan, "sync", dsKvL, krop_s[0:64, :], kv16[KVL:KVF, :])

    dsWoT = mkdsem("sync", "dsWoT")
    wot_lv = dma(plan, "sync", dsWoT, woT_s[:], wo16, transpose=True)

    plan.wait("vector", dsKvL, kv_lv)
    mz = op(plan, "vector", sV, lambda e: e.memset(krop_s[64:65, :], 1.0))
    op(plan, "vector", sV, lambda e: e.memset(o_sum[:], 0.0))
    plan.wait("vector", dsConst, v_const)
    sqb = op(plan, "vector", sV, lambda e: e.tensor_scalar_mul(sqbup_s[:], bqupT_s[:], SCALE))
    vcols = bkvupT_s[:].rearrange("p (h two) -> p h two", two=2)[:, :, 1:2].rearrange(
        "p h one -> p (h one)")
    bv = op(plan, "vector", sV, lambda e: e.reduce_sum(
        bvs_s[:], vcols, axis=mybir.AxisListType.X))
    plan.wait("vector", sV, bv)
    bv16 = op(plan, "vector", sV, lambda e: e.tensor_copy(bvs16_s[:], bvs_s[:]))

    plan.wait("scalar", sV, mz)
    kr2 = op(plan, "scalar", sS, lambda e: e.activation(krop2_s[:], krop_s[0:64, :], AF.Square))

    plan.wait("tensor", dsWoT, wot_lv)
    plan.wait("tensor", sV, bv16)
    for Ht in range(32):
        fn = lambda e, Ht=Ht: e.matmul(
            pb[6][:, Ht:Ht + 1], woT_s[:, Ht * 128:(Ht + 1) * 128], bvs16_s[:],
            start=True, stop=True)
        if Ht == 31:
            bocmm = op(plan, "tensor", sT, fn)
        else:
            opq(plan, "tensor", fn)
    plan.wait("vector", sT, bocmm)
    boc_v = op(plan, "vector", sV, lambda e: e.tensor_add(boc_s[:], pb[6][:, 0:32], boT_s[:]))

    qwh_s = att_es.enter_context(sbL("qwh", [128, 12 * QH], BF16))
    kvwh_s = att_es.enter_context(sbL("kvwh", [128, 4 * 256], BF16))
    k_s = att_es.enter_context(sbL("k_h", [128, S], BF16))
    sq_s = att_es.enter_context(sbL("sqscr", [128, S], BF16))
    v_s = att_es.enter_context(sbL("v_h", [128, 32 * 130], BF16))
    qn_s = att_es.enter_context(sbL("qn_h", [128, T], BF16))
    qr_s = att_es.enter_context(sbL("qr_h", [65, T], BF16))
    pr_s = [att_es.enter_context(sbL(f"pr{k}", [128, 512], BF16)) for k in range(2)]
    rc_s = att_es.enter_context(sbL("rc", [128, 4], F32))

    vz = op(plan, "vector", sV, lambda e: e.memset(
        v_s[:].rearrange("p (kc j) -> p kc j", j=130)[:, :, 128:130], 1.0))

    dsQwh = mkdsem("sync", "dsQwh")
    dsKvwh = mkdsem("sync", "dsKvwh")

    qwh_use = 0
    kvwh_use = 0
    kgen_ev = 0
    vgen_ev = 0
    qgen_ev = 0
    osum_ev = 0
    b6_ev = boc_v
    exp_done = [0, 0]
    attnv_done = [0, 0]
    norm_done = 0

    plan.wait("tensor", sV, vz)
    for h in range(NH):
        plan.wait("sync", sT, qwh_use)
        qsrc = wqupTh.rearrange("p (l f) -> p l f", f=NH * QH)[:, :, h * QH:(h + 1) * QH]
        qw_lv = dma(plan, "sync", dsQwh, qwh_s[:].rearrange("p (l j) -> p l j", j=QH), qsrc)
        plan.wait("sync", sT, kvwh_use)
        ksrc = wkvupTh.rearrange("p (l f) -> p l f", f=NH * 256)[:, :, h * 256:(h + 1) * 256]
        kv_lv2 = dma(plan, "sync", dsKvwh, kvwh_s[:].rearrange("p (l j) -> p l j", j=256), ksrc)

        plan.wait("tensor", dsKvwh, kv_lv2)
        plan.wait("tensor", dsKvL, kv_lv)
        plan.wait("tensor", sV, kgen_ev)
        plan.wait("tensor", sV, osum_ev)
        kev = 0
        for nt in range(8):
            bk = 4 + (nt % 2)
            if nt >= 2:
                plan.wait("tensor", sV, kev)
            for l in range(4):
                fn = lambda e, nt=nt, l=l, bk=bk: e.matmul(
                    pb[bk][:, :], kvwh_s[:, l * 256:l * 256 + 128],
                    kvn_s[l][:, nt * 512:(nt + 1) * 512],
                    start=(l == 0), stop=(l == 3))
                if l == 3:
                    kmm = op(plan, "tensor", sT, fn)
                else:
                    opq(plan, "tensor", fn)
            plan.wait("vector", sT, kmm)
            kev = op(plan, "vector", sV, lambda e, nt=nt, bk=bk, h=h: e.tensor_scalar_add(
                k_s[:, nt * 512:(nt + 1) * 512], pb[bk][:, :],
                bkvupT_s[:, 2 * h:2 * h + 1]))
        kgen_ev = kev

        plan.wait("scalar", sV, kgen_ev)
        plan.wait("scalar", sV, norm_done)
        ksq = op(plan, "scalar", sS, lambda e: e.activation(sq_s[:], k_s[:], AF.Square))
        plan.wait("tensor", sS, ksq)
        plan.wait("tensor", sV, b6_ev)
        kn_ev = 0
        for nt in range(8):
            if nt > 0:
                plan.wait("tensor", sV, kn_ev)
            opq(plan, "tensor", lambda e, nt=nt: e.matmul(
                pb[6][0:1, :], ones_s[:, 0:1], sq_s[:, nt * 512:(nt + 1) * 512],
                start=True, stop=False))
            knm = op(plan, "tensor", sT, lambda e, nt=nt: e.matmul(
                pb[6][0:1, :], ones_s[0:64, 0:1], krop2_s[:, nt * 512:(nt + 1) * 512],
                start=False, stop=True))
            plan.wait("vector", sT, knm)
            kn_ev = op(plan, "vector", sV, lambda e, nt=nt: e.reduce_max(
                kmx_s[0:1, nt:nt + 1], pb[6][0:1, :], axis=mybir.AxisListType.X))
        plan.wait("vector", sV, kn_ev)
        kmax = op(plan, "vector", sV, lambda e: e.reduce_max(
            kmx_s[0:1, 8:9], kmx_s[0:1, 0:8], axis=mybir.AxisListType.X))
        b6_ev = kn_ev

        plan.wait("tensor", sV, vgen_ev)
        vev = 0
        for kc in range(32):
            off = 256 * (kc % 2)
            if kc >= 2:
                plan.wait("tensor", sV, vev)
            for l in range(4):
                fn = lambda e, kc=kc, l=l, off=off: e.matmul(
                    pb[7][:, off:off + 128], kvn_s[l][:, kc * 128:(kc + 1) * 128],
                    kvwh_s[:, l * 256 + 128:l * 256 + 256],
                    start=(l == 0), stop=(l == 3))
                if l == 3:
                    vmm = op(plan, "tensor", sT, fn)
                else:
                    opq(plan, "tensor", fn)
            plan.wait("vector", sT, vmm)
            vev = op(plan, "vector", sV, lambda e, kc=kc, off=off: e.tensor_copy(
                v_s[:, kc * 130:kc * 130 + 128], pb[7][:, off:off + 128]))
        vgen_ev = vev

        plan.wait("tensor", dsQwh, qw_lv)
        plan.wait("tensor", sV, qgen_ev)
        plan.wait("tensor", sV, kmax)
        qev = 0
        qn_last = 0
        for nt in range(4):
            bk = 4 + (nt % 2)
            if nt >= 1:
                plan.wait("tensor", sV, qn_last)
            for l in range(12):
                fn = lambda e, nt=nt, l=l, bk=bk: e.matmul(
                    pb[bk][:, :], qwh_s[:, l * QH:l * QH + NOPE],
                    qc_s[:, l * T + nt * 512:l * T + (nt + 1) * 512],
                    start=(l == 0), stop=(l == 11))
                if l == 11:
                    qmm = op(plan, "tensor", sT, fn)
                else:
                    opq(plan, "tensor", fn)
            for l in range(12):
                fn = lambda e, nt=nt, l=l: e.matmul(
                    pb[6][0:64, :], qwh_s[:, l * QH + NOPE:(l + 1) * QH],
                    qc_s[:, l * T + nt * 512:l * T + (nt + 1) * 512],
                    start=(l == 0), stop=(l == 11))
                if l == 11:
                    qrmm = op(plan, "tensor", sT, fn)
                else:
                    opq(plan, "tensor", fn)
            plan.wait("vector", sT, qrmm)
            plan.wait("vector", sV, sqb)
            opq(plan, "vector", lambda e, nt=nt, bk=bk, h=h: e.tensor_scalar(
                qn_s[:, nt * 512:(nt + 1) * 512], pb[bk][:, :],
                SCALE, sqbup_s[:, 2 * h:2 * h + 1],
                ALU.mult, ALU.add))
            qn_last = op(plan, "vector", sV, lambda e, nt=nt, bk=bk, h=h: e.tensor_scalar(
                qr_s[0:64, nt * 512:(nt + 1) * 512], pb[6][0:64, :],
                SCALE, sqbup_s[0:64, 2 * h + 1:2 * h + 2],
                ALU.mult, ALU.add))
            qev = qn_last
        qgen_ev = qev
        b6_ev = qev

        plan.wait("scalar", sV, qgen_ev)
        qsq = op(plan, "scalar", sS, lambda e: e.activation(
            sq_s[:, 0:T], qn_s[:], AF.Square))
        qsq2 = op(plan, "scalar", sS, lambda e: e.activation(
            sq_s[0:64, T:2 * T], qr_s[0:64, :], AF.Square))
        plan.wait("tensor", sS, qsq2)
        plan.wait("tensor", sV, b6_ev)
        qn_ev = 0
        for nt in range(4):
            if nt > 0:
                plan.wait("tensor", sV, qn_ev)
            opq(plan, "tensor", lambda e, nt=nt: e.matmul(
                pb[6][0:1, :], ones_s[:, 0:1], sq_s[:, nt * 512:(nt + 1) * 512],
                start=True, stop=False))
            qnm = op(plan, "tensor", sT, lambda e, nt=nt: e.matmul(
                pb[6][0:1, :], ones_s[0:64, 0:1],
                sq_s[0:64, T + nt * 512:T + (nt + 1) * 512],
                start=False, stop=True))
            plan.wait("vector", sT, qnm)
            qn_ev = op(plan, "vector", sV, lambda e, nt=nt: e.tensor_copy(
                qn2r_s[0:1, nt * 512:(nt + 1) * 512], pb[6][0:1, :]))
        b6_ev = qn_ev
        plan.wait("vector", sV, qn_ev)
        pr = op(plan, "vector", sV, lambda e: e.tensor_scalar_mul(
            srow_s[:], qn2r_s[:], kmx_s[0:1, 8:9]))
        plan.wait("scalar", sV, pr)
        sq2 = op(plan, "scalar", sS, lambda e: e.activation(srow_s[:], srow_s[:], AF.Sqrt))
        plan.wait("vector", sS, sq2)
        norm_done = op(plan, "vector", sV, lambda e: e.tensor_scalar_mul(
            qr_s[64:65, :], srow_s[:], -1.0))

        plan.wait("tensor", sV, norm_done)
        for tq in range(4):
            for kc in range(32):
                sl = kc % 2
                plan.wait("tensor", sS, exp_done[sl])
                opq(plan, "tensor", lambda e, kc=kc, tq=tq, sl=sl: e.matmul(
                    pb[sl][:, :], k_s[:, kc * 128:(kc + 1) * 128],
                    qn_s[:, tq * 512:(tq + 1) * 512], start=True, stop=False))
                scv = op(plan, "tensor", sT, lambda e, kc=kc, tq=tq, sl=sl: e.matmul(
                    pb[sl][:, :], krop_s[:, kc * 128:(kc + 1) * 128],
                    qr_s[:, tq * 512:(tq + 1) * 512], start=False, stop=True))
                plan.wait("scalar", sT, scv)
                plan.wait("scalar", sT, attnv_done[sl])
                exp_done[sl] = op(plan, "scalar", sS, lambda e, sl=sl: e.activation(
                    pr_s[sl][:], pb[sl][:, :], AF.Exp))
                plan.wait("tensor", sS, exp_done[sl])
                if kc == 0:
                    plan.wait("tensor", sV, osum_ev)
                for qt in range(4):
                    bk = 2 + qt
                    fn = lambda e, kc=kc, qt=qt, bk=bk: e.matmul(
                        pb[bk][:, 0:129],
                        pr_s[kc % 2][:, qt * 128:(qt + 1) * 128],
                        v_s[:, kc * 130:kc * 130 + 129],
                        start=(kc == 0), stop=(kc == 31))
                    if qt == 3:
                        attnv_done[kc % 2] = op(plan, "tensor", sT, fn)
                    else:
                        opq(plan, "tensor", fn)
            plan.wait("vector", sT, attnv_done[1])
            for qt in range(4):
                bk = 2 + qt
                rcv = op(plan, "vector", sV, lambda e, qt=qt, bk=bk: e.reciprocal(
                    rc_s[:, qt:qt + 1], pb[bk][:, 128:129]))
                plan.wait("vector", sV, rcv)
                g = tq * 4 + qt
                osum_ev = op(plan, "vector", sV, lambda e, qt=qt, bk=bk, g=g:
                             e.scalar_tensor_tensor(
                                 o_sum[:, g * VH:(g + 1) * VH], pb[bk][:, 0:VH],
                                 rc_s[:, qt:qt + 1], o_sum[:, g * VH:(g + 1) * VH],
                                 ALU.mult, ALU.add))
        qwh_use = qrmm
        kvwh_use = vmm

    plan.wait("tensor", sV, osum_ev)
    plan.wait("tensor", sS, exp_done[1])
    plan.wait("tensor", sS, exp_done[0])
    ot_ev = 0
    for g in range(16):
        bk = g % 4
        if g >= 4:
            plan.wait("tensor", sV, ot_ev)
        tv = op(plan, "tensor", sT, lambda e, g=g, bk=bk: e.transpose(
            pb[bk][:, 0:128], o_sum[:, g * VH:(g + 1) * VH], id_s[:]))
        plan.wait("vector", sT, tv)
        ot_ev = op(plan, "vector", sV, lambda e, g=g, bk=bk: e.tensor_copy(
            osT[:, g * VH:(g + 1) * VH], pb[bk][:, 0:128]))

    dsX1r = [mkdsem("scalar", f"dsX1r{k}") for k in range(2)]
    dsX2 = [mkdsem("gpsimd", f"dsX2{k}") for k in range(2)]
    with sbL("opx0", [128, T], F32) as opx0, \
         sbL("opx1", [128, T], F32) as opx1, \
         sbL("opo0", [128, T], F32) as opo0, \
         sbL("opo1", [128, T], F32) as opo1:
        opx = [opx0, opx1]
        opo = [opo0, opo1]
        x2st = [0, 0]
        opev = [0, 0]
        plan.wait("vector", sV, boc_v)
        plan.wait("tensor", sV, ot_ev)
        for Ht in range(32):
            sl = Ht % 2
            bk0 = 4 * sl
            plan.wait("scalar", dsX2[sl], x2st[sl])
            plan.wait("scalar", sV, opev[sl])
            xv = dma(plan, "scalar", dsX1r[sl], opx[sl][:],
                     x1T[Ht * 128:(Ht + 1) * 128, :])
            plan.wait("tensor", sV, opev[sl])
            for nt in range(4):
                fn = lambda e, Ht=Ht, nt=nt, bk0=bk0: e.matmul(
                    pb[bk0 + nt][:, :], woT_s[:, Ht * 128:(Ht + 1) * 128],
                    osT[:, nt * 512:(nt + 1) * 512], start=True, stop=True)
                if nt == 3:
                    omm = op(plan, "tensor", sT, fn)
                else:
                    opq(plan, "tensor", fn)
            plan.wait("vector", sT, omm)
            plan.wait("vector", dsX1r[sl], xv)
            for nt in range(4):
                fn = lambda e, Ht=Ht, nt=nt, sl=sl, bk0=bk0: e.scalar_tensor_tensor(
                    opo[sl][:, nt * 512:(nt + 1) * 512], pb[bk0 + nt][:, :],
                    boc_s[:, Ht:Ht + 1], opx[sl][:, nt * 512:(nt + 1) * 512],
                    ALU.add, ALU.add)
                if nt == 3:
                    opev[sl] = op(plan, "vector", sV, fn)
                else:
                    opq(plan, "vector", fn)
            plan.wait("gpsimd", sV, opev[sl])
            x2st[sl] = dma(plan, "gpsimd", dsX2[sl],
                           x2T[Ht * 128:(Ht + 1) * 128, :], opo[sl][:])

    barrier(plan, sB, compute=CMP)
    att_es.close()

def build_core(mode, debug=False):
    assert mode in ("l1", "l2")
    nc = bass.Bass("TRN2", target_bir_lowering=False)

    xin = nc.dram_tensor("xin", [D, T], F32, kind="ExternalInput").ap()
    qw_g = nc.dram_tensor("qw_gate", [D, FF], I32, kind="ExternalInput").ap()
    z_g = nc.dram_tensor("z_gate", [32, FF], I32, kind="ExternalInput").ap()
    s_g = nc.dram_tensor("s_gate", [32, FF], F32, kind="ExternalInput").ap()
    qw_u = nc.dram_tensor("qw_up", [D, FF], I32, kind="ExternalInput").ap()
    z_u = nc.dram_tensor("z_up", [32, FF], I32, kind="ExternalInput").ap()
    s_u = nc.dram_tensor("s_up", [32, FF], F32, kind="ExternalInput").ap()
    qw_d = nc.dram_tensor("qw_down", [FF, D], I32, kind="ExternalInput").ap()
    z_d = nc.dram_tensor("z_down", [86, D], I32, kind="ExternalInput").ap()
    s_d = nc.dram_tensor("s_down", [86, D], F32, kind="ExternalInput").ap()
    id_f = nc.dram_tensor("id_f", [128, 128], F32, kind="ExternalInput").ap()
    ones_b = nc.dram_tensor("ones_b", [128, 128], BF16, kind="ExternalInput").ap()
    epsb = nc.dram_tensor("epsb", [128, 1], F32, kind="ExternalInput").ap()

    if mode == "l1":
        wkvc = nc.dram_tensor("Wkv_c", [640, D], F32, kind="ExternalInput").ap()
        bkvc_t = nc.dram_tensor("bkvc_t", [128, 5], F32, kind="ExternalInput").ap()
        x1T = nc.dram_tensor("x1T", [D, T], F32, kind="ExternalOutput").ap()
        kv_own = nc.dram_tensor("kv_own", [KVF, T], F32, kind="ExternalOutput").ap()
    else:
        kvf = nc.dram_tensor("kv_full", [KVF, S], F32, kind="ExternalInput").ap()
        wqc = nc.dram_tensor("Wq_c", [QL, D], F32, kind="ExternalInput").ap()
        bqc_t = nc.dram_tensor("bqc_t", [128, 12], F32, kind="ExternalInput").ap()
        wqup = nc.dram_tensor("Wq_up", [NH * QH, QL], F32, kind="ExternalInput").ap()
        bqup_t = nc.dram_tensor("bqup_t", [128, 64], F32, kind="ExternalInput").ap()
        wkvup = nc.dram_tensor("Wkv_up", [NH * 256, KVL], F32, kind="ExternalInput").ap()
        bkvup_t = nc.dram_tensor("bkvup_t", [128, 64], F32, kind="ExternalInput").ap()
        wo = nc.dram_tensor("Wo", [D, VH], F32, kind="ExternalInput").ap()
        bo_t = nc.dram_tensor("bo_t", [128, 32], F32, kind="ExternalInput").ap()
        outT = nc.dram_tensor("outT", [D, T], F32, kind="ExternalOutput").ap()
        x2T = nc.dram_tensor("x2T", [D, T], F32, kind="Internal").ap()
        xn2h = nc.dram_tensor("xn2h", [D, T], BF16, kind="Internal").ap()

    qwg16 = nc.dram_tensor("qwg16", [D, FF], BF16, kind="Internal").ap()
    qwu16 = nc.dram_tensor("qwu16", [D, FF], BF16, kind="Internal").ap()
    qwd16 = nc.dram_tensor("qwd16", [FF, D], BF16, kind="Internal").ap()
    dTg = nc.dram_tensor("dTg", [FF, D], BF16, kind="Internal").ap()
    dTu = nc.dram_tensor("dTu", [FF, D], BF16, kind="Internal").ap()
    dTd = nc.dram_tensor("dTd", [D, FF], BF16, kind="Internal").ap()
    wgB = nc.dram_tensor("wgB", [43, D, 256], BF16, kind="Internal").ap()
    wuB = nc.dram_tensor("wuB", [43, D, 256], BF16, kind="Internal").ap()
    wdB = nc.dram_tensor("wdB", [16, FF, 256], BF16, kind="Internal").ap()
    hact = nc.dram_tensor("hact", [FF, T], BF16, kind="Internal").ap()

    es = ExitStack()
    plan = Plan(nc, es)

    sV = Ctr(es.enter_context(nc.semaphore("sV")))
    sS = Ctr(es.enter_context(nc.semaphore("sS")))
    sT = Ctr(es.enter_context(nc.semaphore("sT")))
    sB = Ctr(es.enter_context(nc.semaphore("sB")))
    CMP = (("vector", sV), ("scalar", sS), ("tensor", sT))

    pb = [es.enter_context(nc.psum_tensor(f"pb{i}", [128, 512], F32)) for i in range(8)]

    _uid = [0]

    def sbL(name, shape, dt):
        _uid[0] += 1
        return nc.sbuf_tensor(f"{name}_{_uid[0]}", shape, dt, side="left")

    def sbR(name, shape, dt):
        _uid[0] += 1
        return nc.sbuf_tensor(f"{name}_{_uid[0]}", shape, dt, side="right")

    id_s = es.enter_context(sbL("id_s", [128, 128], F32))
    ones_s = es.enter_context(sbL("ones_s", [128, 128], BF16))
    epsb_s = es.enter_context(sbL("epsb_s", [128, 1], F32))

    dsConst = plan.dsem("sync", "dsConst")
    dma(plan, "sync", dsConst, id_s[:], id_f)
    dma(plan, "sync", dsConst, ones_s[:], ones_b)
    v_const = dma(plan, "sync", dsConst, epsb_s[:], epsb)
    if mode == "l1":
        bkvcT_s = es.enter_context(sbL("bkvcT_s", [128, 5], F32))
        v_const = dma(plan, "sync", dsConst, bkvcT_s[:], bkvc_t)
    else:
        bqcT_s = es.enter_context(sbL("bqcT_s", [128, 12], F32))
        bqupT_s = es.enter_context(sbL("bqupT_s", [128, 64], F32))
        bkvupT_s = es.enter_context(sbL("bkvupT_s", [128, 64], F32))
        boT_s = es.enter_context(sbL("boT_s", [128, 32], F32))
        dma(plan, "sync", dsConst, bqcT_s[:], bqc_t)
        dma(plan, "sync", dsConst, bqupT_s[:], bqup_t)
        dma(plan, "sync", dsConst, bkvupT_s[:], bkvup_t)
        v_const = dma(plan, "sync", dsConst, boT_s[:], bo_t)

    dsCast = {m: plan.dsem("gpsimd", f"dsCast_{m}") for m in ("g", "u", "d", "a")}
    cast_done = {}
    cast_done["g"] = dma(plan, "gpsimd", dsCast["g"], qwg16, qw_g)
    cast_done["u"] = dma(plan, "gpsimd", dsCast["u"], qwu16, qw_u)
    cast_done["d"] = dma(plan, "gpsimd", dsCast["d"], qwd16, qw_d)
    if mode == "l1":
        wkvc16 = nc.dram_tensor("wkvc16", [640, D], BF16, kind="Internal").ap()
        wkvcTh = nc.dram_tensor("wkvcTh", [128, NCD * 640], BF16, kind="Internal").ap()
        cast_done["a"] = dma(plan, "gpsimd", dsCast["a"], wkvc16, wkvc)
    else:
        wqc16 = nc.dram_tensor("wqc16", [QL, D], BF16, kind="Internal").ap()
        wqup16 = nc.dram_tensor("wqup16", [NH * QH, QL], BF16, kind="Internal").ap()
        wkvup16 = nc.dram_tensor("wkvup16", [NH * 256, KVL], BF16, kind="Internal").ap()
        wo16 = nc.dram_tensor("wo16", [D, VH], BF16, kind="Internal").ap()
        kv16 = nc.dram_tensor("kv16", [KVF, S], BF16, kind="Internal").ap()
        wqcTh = nc.dram_tensor("wqcTh", [128, NCD * QL], BF16, kind="Internal").ap()
        wqupTh = nc.dram_tensor("wqupTh", [128, 12 * NH * QH], BF16, kind="Internal").ap()
        wkvupTh = nc.dram_tensor("wkvupTh", [128, 4 * NH * 256], BF16, kind="Internal").ap()
        dma(plan, "gpsimd", dsCast["a"], wqc16, wqc)
        dma(plan, "gpsimd", dsCast["a"], wqup16, wqup)
        dma(plan, "gpsimd", dsCast["a"], wkvup16, wkvup)
        dma(plan, "gpsimd", dsCast["a"], wo16, wo)
        cast_done["a"] = dma(plan, "gpsimd", dsCast["a"], kv16, kvf)

    zs_es = ExitStack()
    zgT = zs_es.enter_context(sbL("zgT", [128, 86 * 32], F32))
    sgT = zs_es.enter_context(sbL("sgT", [128, 86 * 32], F32))
    zuT = zs_es.enter_context(sbL("zuT", [128, 86 * 32], F32))
    suT = zs_es.enter_context(sbL("suT", [128, 86 * 32], F32))
    zdT = zs_es.enter_context(sbL("zdT", [128, 32 * 86], F32))
    sdT = zs_es.enter_context(sbL("sdT", [128, 32 * 86], F32))

    zsraw_es = ExitStack()
    zraw_sh = zsraw_es.enter_context(sbL("zs_zraw", [86, FF], F32))
    sraw_sh = zsraw_es.enter_context(sbL("zs_sraw", [86, FF], F32))
    dsZraw = plan.dsem("gpsimd", "dsZraw")
    dsSraw = plan.dsem("sync", "dsSraw")
    zs_state = {"last_pe": 0, "prev_ev": [0, 0, 0, 0]}

    def zs_prep(zdram, sdram, zT_s, sT_s, rows, ncols_t, tile_w):
        plan.wait("gpsimd", sT, zs_state["last_pe"])
        plan.wait("sync", sT, zs_state["last_pe"])
        vz = dma(plan, "gpsimd", dsZraw, zraw_sh[0:rows, 0:ncols_t * 128], zdram)
        vs = dma(plan, "sync", dsSraw, sraw_sh[0:rows, 0:ncols_t * 128], sdram)
        plan.wait("tensor", dsZraw, vz)
        plan.wait("tensor", dsSraw, vs)
        plan.wait("tensor", dsConst, v_const)
        prev_ev = zs_state["prev_ev"]
        for i in range(ncols_t):
            bk = 4 + (i % 4)
            plan.wait("tensor", sV, prev_ev[i % 4])
            opq(plan, "tensor", lambda e, i=i, bk=bk: e.transpose(
                pb[bk][:, 0:tile_w], zraw_sh[0:rows, i * 128:(i + 1) * 128],
                id_s[0:rows, 0:rows]))
            tv = op(plan, "tensor", sT, lambda e, i=i, bk=bk: e.transpose(
                pb[bk][:, 128:128 + tile_w], sraw_sh[0:rows, i * 128:(i + 1) * 128],
                id_s[0:rows, 0:rows]))
            zs_state["last_pe"] = tv
            plan.wait("vector", sT, tv)
            opq(plan, "vector", lambda e, i=i, bk=bk: e.tensor_copy(
                zT_s[:, i * tile_w:(i + 1) * tile_w], pb[bk][:, 0:tile_w]))
            prev_ev[i % 4] = op(plan, "vector", sV, lambda e, i=i, bk=bk: e.tensor_copy(
                sT_s[:, i * tile_w:(i + 1) * tile_w], pb[bk][:, 128:128 + tile_w]))

    zs_prep(z_g, s_g, zgT, sgT, 32, 86, 32)
    zs_prep(z_u, s_u, zuT, suT, 32, 86, 32)
    zs_prep(z_d, s_d, zdT, sdT, 86, 32, 86)
    zsraw_es.close()

    barrier(plan, sB, compute=CMP)

    dsXb = [plan.dsem("sync", f"dsXb{k}") for k in range(3)]

    def emit_stats(x_src, mb_out, invb_out):
        with sbL("st_scr", [128, T], F32) as scr, \
             sbL("st_xb0", [128, T], F32) as xb0, \
             sbL("st_xb1", [128, T], F32) as xb1, \
             sbL("st_xb2", [128, T], F32) as xb2, \
             sbL("st_x16a", [128, T], BF16) as x16a, \
             sbL("st_x16b", [128, T], BF16) as x16b, \
             sbL("st_xsqa", [128, T], BF16) as xsqa, \
             sbL("st_xsqb", [128, T], BF16) as xsqb:
            xb = [xb0, xb1, xb2]
            x16 = [x16a, x16b]
            xsq = [xsqa, xsqb]
            use_pe = [0, 0, 0]
            use2 = [0, 0]
            for c in range(NCD):
                sl = c % 3
                s2 = c % 2
                plan.wait("sync", sT, use_pe[sl])
                lv = dma(plan, "sync", dsXb[sl], xb[sl][:], x_src[c * 128:(c + 1) * 128, :])
                plan.wait("vector", dsXb[sl], lv)
                plan.wait("vector", sT, use2[s2])
                cv = op(plan, "vector", sV, lambda e, sl=sl, s2=s2: e.tensor_copy(x16[s2][:], xb[sl][:]))
                plan.wait("scalar", dsXb[sl], lv)
                plan.wait("scalar", sT, use2[s2])
                qv = op(plan, "scalar", sS, lambda e, sl=sl, s2=s2: e.activation(
                    xsq[s2][:], xb[sl][:], AF.Square))
                plan.wait("tensor", sV, cv)
                plan.wait("tensor", sS, qv)
                for nt in range(4):
                    opq(plan, "tensor", lambda e, nt=nt, s2=s2, c=c: e.matmul(
                        pb[nt][:, :], ones_s[:], x16[s2][:, nt * 512:(nt + 1) * 512],
                        start=(c == 0), stop=(c == NCD - 1)))
                for nt in range(4):
                    fn = lambda e, nt=nt, s2=s2, c=c: e.matmul(
                        pb[4 + nt][:, :], ones_s[:], xsq[s2][:, nt * 512:(nt + 1) * 512],
                        start=(c == 0), stop=(c == NCD - 1))
                    if nt == 3:
                        use_pe[sl] = use2[s2] = op(plan, "tensor", sT, fn)
                    else:
                        opq(plan, "tensor", fn)
            pe_done = use_pe[(NCD - 1) % 3]
            plan.wait("scalar", sT, pe_done)
            for nt in range(4):
                opq(plan, "scalar", lambda e, nt=nt: e.activation(
                    mb_out[:, nt * 512:(nt + 1) * 512], pb[nt][:, :], AF.Copy, scale=1.0 / D))
            for nt in range(4):
                ev = op(plan, "scalar", sS, lambda e, nt=nt: e.activation(
                    scr[:, nt * 512:(nt + 1) * 512], pb[4 + nt][:, :], AF.Copy, scale=1.0 / D))
            plan.wait("vector", sS, ev)
            mv = op(plan, "vector", sV, lambda e: e.tensor_mul(invb_out[:], mb_out[:], mb_out[:]))
            plan.wait("vector", sV, mv)
            vv = op(plan, "vector", sV, lambda e: e.tensor_sub(scr[:], scr[:], invb_out[:]))
            plan.wait("scalar", sV, vv)
            svv = op(plan, "scalar", sS, lambda e: e.activation(
                scr[:], scr[:], AF.Sqrt, bias=epsb_s[:, 0:1]))
            plan.wait("vector", sS, svv)
            return op(plan, "vector", sV, lambda e: e.reciprocal(invb_out[:], scr[:]))

    dq_es = ExitStack()
    dqi = [dq_es.enter_context(sbL(f"dqi{k}", [128, FF], BF16)) for k in range(2)]
    dqo = [dq_es.enter_context(sbL(f"dqo{k}", [128, FF], BF16)) for k in range(2)]
    dsDqi = [plan.dsem("sync", f"dsDqi{k}") for k in range(2)]
    dsDqo = [plan.dsem("scalar", f"dsDqo{k}") for k in range(2)]
    dq_state = {"use_v": [0, 0], "use_st": [0, 0]}

    def dequant(qw16, dT_out, zT_s, sT_s, nslices, width, ngroups, dsC, cast_v):
        use_v = dq_state["use_v"]
        use_st = dq_state["use_st"]
        plan.wait("sync", dsC, cast_v)
        for i in range(nslices):
            sl = i % 2
            plan.wait("sync", sV, use_v[sl])
            lv = dma(plan, "sync", dsDqi[sl], dqi[sl][:, 0:width],
                     qw16[:, i * 128:(i + 1) * 128], transpose=True)
            plan.wait("vector", dsDqi[sl], lv)
            plan.wait("vector", dsDqo[sl], use_st[sl])
            for g in range(ngroups):
                fn = lambda e, sl=sl, g=g, i=i: e.tensor_scalar(
                    dqo[sl][:, g * 128:(g + 1) * 128], dqi[sl][:, g * 128:(g + 1) * 128],
                    zT_s[:, i * ngroups + g:i * ngroups + g + 1],
                    sT_s[:, i * ngroups + g:i * ngroups + g + 1],
                    ALU.subtract, ALU.mult)
                if g == ngroups - 1:
                    use_v[sl] = op(plan, "vector", sV, fn)
                else:
                    opq(plan, "vector", fn)
            plan.wait("scalar", sV, use_v[sl])
            use_st[sl] = dma(plan, "scalar", dsDqo[sl], dT_out[i * 128:(i + 1) * 128, :],
                             dqo[sl][:, 0:width])

    dequant(qwg16, dTg, zgT, sgT, 86, D, 32, dsCast["g"], cast_done["g"])
    dequant(qwu16, dTu, zuT, suT, 86, D, 32, dsCast["u"], cast_done["u"])
    dequant(qwd16, dTd, zdT, sdT, 32, FF, 86, dsCast["d"], cast_done["d"])

    with sbL("awl0", [128, FF], BF16) as awl0, \
         sbL("awl1", [128, FF], BF16) as awl1:
        awl = [awl0, awl1]
        dsAwl = [plan.dsem("sync", f"dsAwl{k}") for k in range(2)]
        dsAwSt = [plan.dsem("scalar", f"dsAwSt{k}") for k in range(2)]
        aw_st = [0, 0]
        aw_i = [0]

        def aw_relayout(src16, srcrows, dst, ncols, width):
            for c in range(ncols):
                sl = aw_i[0] % 2
                aw_i[0] += 1
                plan.wait("sync", dsAwSt[sl], aw_st[sl])
                lv = dma(plan, "sync", dsAwl[sl], awl[sl][:, 0:width],
                         src16[0:srcrows, c * 128:(c + 1) * 128], transpose=True)
                plan.wait("scalar", dsAwl[sl], lv)
                aw_st[sl] = dma(plan, "scalar", dsAwSt[sl],
                                dst[:, c * width:(c + 1) * width], awl[sl][:, 0:width])

        plan.wait("sync", dsCast["a"], cast_done["a"])
        if mode == "l1":
            aw_relayout(wkvc16, 640, wkvcTh, NCD, 640)
        else:
            aw_relayout(wqc16, QL, wqcTh, NCD, QL)
            aw_relayout(wqup16, NH * QH, wqupTh, 12, NH * QH)
            aw_relayout(wkvup16, NH * 256, wkvupTh, 4, NH * 256)

    barrier(plan, sB, compute=CMP)
    dq_es.close()
    zs_es.close()

    rl_es = ExitStack()
    rl = [rl_es.enter_context(sbL(f"rl{k}", [128, 2 * FF], BF16)) for k in range(2)]
    dsRl = [plan.dsem("sync", f"dsRl{k}") for k in range(2)]
    dsRlw = [plan.dsem("gpsimd", f"dsRlw{k}") for k in range(2)]
    rl_state = {"use_w": [0, 0]}

    def relayout(dT_in, wB_out, ngrp, nblk, width):
        use_w = rl_state["use_w"]
        last = 0
        for grp in range(ngrp):
            sl = grp % 2
            plan.wait("sync", dsRlw[sl], use_w[sl])
            lv = 0
            for k in range(2):
                c = grp * 2 + k
                lv = dma(plan, "sync", dsRl[sl], rl[sl][:, k * width:k * width + width],
                         dT_in[:, c * 128:(c + 1) * 128], transpose=True)
            plan.wait("gpsimd", dsRl[sl], lv)
            for blk in range(nblk):
                src = rl[sl][:, 0:2 * width].rearrange("p (k f) -> p k f", k=2)[
                    :, :, blk * 256:(blk + 1) * 256]
                dst = wB_out[blk, grp * 256:(grp + 1) * 256, :].rearrange(
                    "(k p) j -> p k j", k=2)
                last = dma(plan, "gpsimd", dsRlw[sl], dst, src)
            use_w[sl] = last
        return last

    relayout(dTg, wgB, 16, 43, FF)
    relayout(dTu, wuB, 16, 43, FF)
    relayout(dTd, wdB, 43, 16, D)

    barrier(plan, sB, compute=CMP)
    rl_es.close()

    dsXnb = [plan.dsem("scalar", f"dsXnb{k}") for k in range(3)]

    def emit_xn(x_src, mb, invb, xn_dst):
        with sbL("xnb0", [128, T], F32) as xnb0, \
             sbL("xnb1", [128, T], F32) as xnb1:
            xnb = [xnb0, xnb1]
            use_v = [0, 0]
            last = 0
            for c in range(NCD):
                sl = c % 2
                plan.wait("scalar", sV, use_v[sl])
                lv = dma(plan, "scalar", dsXnb[sl], xnb[sl][:], x_src[c * 128:(c + 1) * 128, :])
                plan.wait("vector", dsXnb[sl], lv)
                sv_ = op(plan, "vector", sV, lambda e, sl=sl: e.tensor_sub(xnb[sl][:], xnb[sl][:], mb[:]))
                plan.wait("vector", sV, sv_)
                last = use_v[sl] = op(plan, "vector", sV, lambda e, sl=sl, c=c: e.tensor_mul(
                    xn_dst[:, c * T:(c + 1) * T], xnb[sl][:], invb[:]))
        return last

    dsXnSt = [plan.dsem("gpsimd", f"dsXnSt{k}") for k in range(2)]

    def emit_xn_hbm(x_src, mb, invb, dst_dram):
        with sbL("xnb0", [128, T], F32) as xnb0, \
             sbL("xnb1", [128, T], F32) as xnb1, \
             sbL("xno0", [128, T], BF16) as xno0, \
             sbL("xno1", [128, T], BF16) as xno1:
            xnb = [xnb0, xnb1]
            xno = [xno0, xno1]
            use_v = [0, 0]
            st_v = [0, 0]
            last = 0
            for c in range(NCD):
                sl = c % 2
                plan.wait("scalar", sV, use_v[sl])
                lv = dma(plan, "scalar", dsXnb[sl], xnb[sl][:], x_src[c * 128:(c + 1) * 128, :])
                plan.wait("vector", dsXnb[sl], lv)
                sv_ = op(plan, "vector", sV, lambda e, sl=sl: e.tensor_sub(xnb[sl][:], xnb[sl][:], mb[:]))
                plan.wait("vector", sV, sv_)
                plan.wait("vector", dsXnSt[sl], st_v[sl])
                use_v[sl] = op(plan, "vector", sV, lambda e, sl=sl: e.tensor_mul(
                    xno[sl][:], xnb[sl][:], invb[:]))
                plan.wait("gpsimd", sV, use_v[sl])
                st_v[sl] = last = dma(plan, "gpsimd", dsXnSt[sl],
                                      dst_dram[c * 128:(c + 1) * 128, :], xno[sl][:])
        return last

    dsWs = [plan.dsem("sync", f"dsWs{k}") for k in range(3)]
    dsHt = [plan.dsem("gpsimd", f"dsHt{k}") for k in range(2)]

    def emit_gateup(xn_src, wgB_in, wuB_in, hact_out):
        with sbL("ws0", [128, 32 * 256], BF16) as ws0, \
             sbL("ws1", [128, 32 * 256], BF16) as ws1, \
             sbL("ws2", [128, 32 * 256], BF16) as ws2, \
             sbL("gu_ga", [128, T], BF16) as ga, \
             sbL("gu_ua", [128, T], BF16) as ua, \
             sbL("gu_ht0", [128, T], BF16) as ht0, \
             sbL("gu_ht1", [128, T], BF16) as ht1:
            ws = [ws0, ws1, ws2]
            use_pe = [0, 0, 0]
            ldv = [0, 0, 0]
            htb = [ht0, ht1]
            use_ht = [0, 0]
            silu_done = 0
            ucopy_done = 0
            mul_done = 0
            st_done = 0
            for hp in range(43):
                sg_ = (2 * hp) % 3
                su_ = (2 * hp + 1) % 3
                plan.wait("sync", sT, use_pe[sg_])
                ldv[sg_] = dma(plan, "sync", dsWs[sg_],
                               ws[sg_][:].rearrange("p (c j) -> p c j", j=256),
                               wgB_in[hp].rearrange("(c p) j -> p c j", p=128))
                plan.wait("sync", sT, use_pe[su_])
                ldv[su_] = dma(plan, "sync", dsWs[su_],
                               ws[su_][:].rearrange("p (c j) -> p c j", j=256),
                               wuB_in[hp].rearrange("(c p) j -> p c j", p=128))
                for sub in range(2):
                    ht = hp * 2 + sub
                    plan.wait("tensor", dsWs[sg_], ldv[sg_])
                    plan.wait("tensor", sS, silu_done)
                    for c in range(NCD):
                        for nt in range(4):
                            fn = lambda e, c=c, nt=nt, sg_=sg_, sub=sub: e.matmul(
                                pb[nt][:, :],
                                ws[sg_][:, c * 256 + sub * 128:c * 256 + (sub + 1) * 128],
                                xn_src[:, c * T + nt * 512:c * T + (nt + 1) * 512],
                                start=(c == 0), stop=(c == NCD - 1))
                            if c == NCD - 1 and nt == 3:
                                gmm = op(plan, "tensor", sT, fn)
                            else:
                                opq(plan, "tensor", fn)
                    plan.wait("tensor", dsWs[su_], ldv[su_])
                    plan.wait("tensor", sV, ucopy_done)
                    for c in range(NCD):
                        for nt in range(4):
                            fn = lambda e, c=c, nt=nt, su_=su_, sub=sub: e.matmul(
                                pb[4 + nt][:, :],
                                ws[su_][:, c * 256 + sub * 128:c * 256 + (sub + 1) * 128],
                                xn_src[:, c * T + nt * 512:c * T + (nt + 1) * 512],
                                start=(c == 0), stop=(c == NCD - 1))
                            if c == NCD - 1 and nt == 3:
                                umm = op(plan, "tensor", sT, fn)
                            else:
                                opq(plan, "tensor", fn)
                    if sub == 1:
                        use_pe[sg_] = use_pe[su_] = umm
                    plan.wait("scalar", sT, gmm)
                    plan.wait("scalar", sV, mul_done)
                    for nt in range(4):
                        fn = lambda e, nt=nt: e.activation(
                            ga[:, nt * 512:(nt + 1) * 512], pb[nt][:, :], AF.Silu)
                        if nt == 3:
                            silu_done = op(plan, "scalar", sS, fn)
                        else:
                            opq(plan, "scalar", fn)
                    plan.wait("vector", sT, umm)
                    plan.wait("vector", sV, mul_done)
                    for nt in range(4):
                        fn = lambda e, nt=nt: e.tensor_copy(
                            ua[:, nt * 512:(nt + 1) * 512], pb[4 + nt][:, :])
                        if nt == 3:
                            ucopy_done = op(plan, "vector", sV, fn)
                        else:
                            opq(plan, "vector", fn)
                    hsl = ht % 2
                    plan.wait("vector", sS, silu_done)
                    plan.wait("vector", sV, ucopy_done)
                    plan.wait("vector", dsHt[hsl], use_ht[hsl])
                    mul_done = op(plan, "vector", sV, lambda e, hsl=hsl: e.tensor_mul(
                        htb[hsl][:], ga[:], ua[:]))
                    plan.wait("gpsimd", sV, mul_done)
                    use_ht[hsl] = st_done = dma(plan, "gpsimd", dsHt[hsl],
                                                hact_out[ht * 128:(ht + 1) * 128, :],
                                                htb[hsl][:])
        return st_done

    dsHq = plan.dsem("sync", "dsHq")
    dsWd = [plan.dsem("sync", f"dsWd{k}") for k in range(2)]
    dsXr = [plan.dsem("scalar", f"dsXr{k}") for k in range(2)]
    dsX1 = [plan.dsem("gpsimd", f"dsX1{k}") for k in range(2)]

    def emit_down(hact_in, wB_in, xres_src, xout_dst, half_scale):
        with sbL("dn_hq", [128, 86 * 512], BF16) as hq, \
             sbL("dn_wd0", [128, 86 * 256], BF16) as wd0, \
             sbL("dn_wd1", [128, 86 * 256], BF16) as wd1, \
             sbL("dn_xrb0", [128, 512], F32) as xrb0, \
             sbL("dn_xrb1", [128, 512], F32) as xrb1, \
             sbL("dn_x1b0", [128, 512], F32) as x1b0, \
             sbL("dn_x1b1", [128, 512], F32) as x1b1:
            wd = [wd0, wd1]
            xrb = [xrb0, xrb1]
            x1b = [x1b0, x1b1]
            use_wd = [0, 0]
            use_x1 = [0, 0]
            hq_use = 0
            st_done = 0
            evac_done = 0
            for tq in range(4):
                plan.wait("sync", sT, hq_use)
                lv = 0
                for part in range(8):
                    h0 = part * 11
                    h1 = min(86, h0 + 11)
                    if h0 >= 86:
                        break
                    src = hact_in.rearrange("(hc p) t -> p hc t", p=128)[
                        :, h0:h1, tq * 512:(tq + 1) * 512]
                    dstv = hq[:, h0 * 512:h1 * 512].rearrange("p (hc j) -> p hc j", j=512)
                    lv = dma(plan, "sync", dsHq, dstv, src)
                hq_v = lv
                for dp in range(16):
                    sl = dp % 2
                    plan.wait("sync", sT, use_wd[sl])
                    wv = dma(plan, "sync", dsWd[sl],
                             wd[sl][:].rearrange("p (hc j) -> p hc j", j=256),
                             wB_in[dp].rearrange("(hc p) j -> p hc j", p=128))
                    plan.wait("tensor", dsWd[sl], wv)
                    plan.wait("tensor", dsHq, hq_v)
                    plan.wait("tensor", sV, evac_done)
                    for hc in range(86):
                        for k in range(2):
                            fn = lambda e, hc=hc, k=k, sl=sl: e.matmul(
                                pb[k][:, :],
                                wd[sl][:, hc * 256 + k * 128:hc * 256 + (k + 1) * 128],
                                hq[:, hc * 512:(hc + 1) * 512],
                                start=(hc == 0), stop=(hc == 85))
                            if hc == 85 and k == 1:
                                use_wd[sl] = mmv = op(plan, "tensor", sT, fn)
                            else:
                                opq(plan, "tensor", fn)
                    if dp == 15:
                        hq_use = mmv
                    for k in range(2):
                        dt = dp * 2 + k
                        xsl = dt % 2
                        plan.wait("scalar", dsX1[xsl], use_x1[xsl])
                        xv = dma(plan, "scalar", dsXr[xsl], xrb[xsl][:],
                                 xres_src[dt * 128:(dt + 1) * 128, tq * 512:(tq + 1) * 512])
                        plan.wait("vector", dsXr[xsl], xv)
                        plan.wait("vector", sT, mmv)
                        ev = op(plan, "vector", sV, lambda e, k=k, xsl=xsl, hs=half_scale:
                                e.scalar_tensor_tensor(x1b[xsl][:], pb[k][:, :], hs,
                                                       xrb[xsl][:], ALU.mult, ALU.add))
                        if k == 1:
                            evac_done = ev
                        plan.wait("gpsimd", sV, ev)
                        use_x1[xsl] = st_done = dma(
                            plan, "gpsimd", dsX1[xsl],
                            xout_dst[dt * 128:(dt + 1) * 128, tq * 512:(tq + 1) * 512],
                            x1b[xsl][:])
            return st_done

    if mode == "l1":
        ln1_es = ExitStack()
        mb1 = ln1_es.enter_context(sbR("mb1", [128, T], F32))
        invb1 = ln1_es.enter_context(sbR("invb1", [128, T], F32))
        emit_stats(xin, mb1, invb1)
        barrier(plan, sB, compute=CMP)

        xn_es = ExitStack()
        xn = xn_es.enter_context(sbL("xn", [128, NCD * T], BF16))
        xn_done = emit_xn(xin, mb1, invb1, xn)
        ln1_es.close()

        plan.wait("tensor", sV, xn_done)
        plan.wait("sync", sV, xn_done)
        emit_gateup(xn, wgB, wuB, hact)
        barrier(plan, sB, compute=CMP)
        xn_es.close()

        emit_down(hact, wdB, xin, x1T, 0.5)
        barrier(plan, sB, compute=CMP)

        ln2_es = ExitStack()
        mb2 = ln2_es.enter_context(sbR("mb2", [128, T], F32))
        invb2 = ln2_es.enter_context(sbR("invb2", [128, T], F32))
        emit_stats(x1T, mb2, invb2)
        barrier(plan, sB, compute=CMP)

        xn2_es = ExitStack()
        xn2 = xn2_es.enter_context(sbL("xn2", [128, NCD * T], BF16))
        xn2_done = emit_xn(x1T, mb2, invb2, xn2)

        wkvcT = xn2_es.enter_context(sbL("wkvcT", [128, NCD * 640], BF16))
        dsKvT = plan.dsem("sync", "dsKvT")
        plan.wait("sync", sV, xn2_done)
        kvt_v = dma(plan, "sync", dsKvT, wkvcT[:], wkvcTh)
        plan.wait("tensor", dsKvT, kvt_v)
        plan.wait("tensor", sV, xn2_done)
        dsKvSt = plan.dsem("gpsimd", "dsKvSt")
        with sbL("kvb", [128, 512], F32) as kvb:
            kv_st = 0
            ev_bank = [0] * 5
            for tq in range(4):
                for ft in range(5):
                    fw = 128 if ft < 4 else 64
                    plan.wait("tensor", sV, ev_bank[ft])
                    for c in range(NCD):
                        fn = lambda e, c=c, ft=ft, fw=fw, tq=tq: e.matmul(
                            pb[ft][0:fw, :],
                            wkvcT[:, c * 640 + ft * 128:c * 640 + ft * 128 + fw],
                            xn2[:, c * T + tq * 512:c * T + (tq + 1) * 512],
                            start=(c == 0), stop=(c == NCD - 1))
                        if c == NCD - 1:
                            mmv = op(plan, "tensor", sT, fn)
                        else:
                            opq(plan, "tensor", fn)
                    plan.wait("vector", sT, mmv)
                    plan.wait("vector", dsKvSt, kv_st)
                    ev_bank[ft] = op(plan, "vector", sV, lambda e, ft=ft, fw=fw:
                                     e.tensor_scalar_add(kvb[0:fw, :], pb[ft][0:fw, :],
                                                         bkvcT_s[0:fw, ft:ft + 1]))
                    plan.wait("gpsimd", sV, ev_bank[ft])
                    kv_st = dma(plan, "gpsimd", dsKvSt,
                                kv_own[ft * 128:ft * 128 + fw, tq * 512:(tq + 1) * 512],
                                kvb[0:fw, :])
        xn2_es.close()
        ln2_es.close()
    else:
        emit_l2_attention(
            nc, plan, es, sbL, sbR, pb, sV, sS, sT, sB, CMP, id_s, ones_s,
            bqcT_s, bqupT_s, bkvupT_s, boT_s,
            xin, x2T, kv16, wqcTh, wqupTh, wkvupTh, wo16, xn2h,
            emit_stats, emit_xn, emit_xn_hbm, plan.dsem, dsConst, v_const,
            dsCast, cast_done, dma, barrier)

        ln3_es = ExitStack()
        mb3 = ln3_es.enter_context(sbR("mb3", [128, T], F32))
        invb3 = ln3_es.enter_context(sbR("invb3", [128, T], F32))
        emit_stats(x2T, mb3, invb3)
        barrier(plan, sB, compute=CMP)

        xn3_es = ExitStack()
        xn3 = xn3_es.enter_context(sbL("xn3", [128, NCD * T], BF16))
        xn3_done = emit_xn(x2T, mb3, invb3, xn3)
        ln3_es.close()
        plan.wait("tensor", sV, xn3_done)
        plan.wait("sync", sV, xn3_done)
        emit_gateup(xn3, wgB, wuB, hact)
        barrier(plan, sB, compute=CMP)
        xn3_es.close()
        emit_down(hact, wdB, x2T, outT, 0.5)

    for eng, c in plan.dsems:
        plan.wait(eng, c, c.v)

    with nc.Block() as block:
        plan.run(block)
    es.close()
    return nc

def build_l1(debug=False):
    return build_core("l1", debug)


def build_l2(debug=False):
    return build_core("l2", debug)


def _pad_bias(b, rows, cols=128):
    out = np.zeros((rows, cols), dtype=np.float32)
    flat = np.asarray(b, dtype=np.float32).reshape(-1)
    out.reshape(-1)[: flat.size] = flat
    return out


def _common_maps(inputs):
    return {
        "qw_gate": np.asarray(inputs["L0_ffn_qw_gate"]),
        "z_gate": np.asarray(inputs["L0_ffn_z_gate"]),
        "s_gate": np.asarray(inputs["L0_ffn_s_gate"]),
        "qw_up": np.asarray(inputs["L0_ffn_qw_up"]),
        "z_up": np.asarray(inputs["L0_ffn_z_up"]),
        "s_up": np.asarray(inputs["L0_ffn_s_up"]),
        "qw_down": np.asarray(inputs["L0_ffn_qw_down"]),
        "z_down": np.asarray(inputs["L0_ffn_z_down"]),
        "s_down": np.asarray(inputs["L0_ffn_s_down"]),
        "id_f": np.eye(128, dtype=np.float32),
        "ones_b": np.ones((128, 128), dtype=ml_dtypes.bfloat16),
        "epsb": np.full((128, 1), EPS, dtype=np.float32),
    }


def l1_in_maps(inputs):
    x = np.asarray(inputs["L0_attn_hidden_size"])
    common = _common_maps(inputs)
    common["Wkv_c"] = np.concatenate(
        [np.asarray(inputs["L0_attn_Wkv_c"], dtype=np.float32),
         np.zeros((64, D), np.float32)], axis=0)
    common["bkvc_t"] = np.ascontiguousarray(_pad_bias(inputs["L0_attn_bkv_c"], 5).T)
    maps = []
    for c in range(8):
        b, half = divmod(c, 2)
        m = dict(common)
        m["xin"] = np.ascontiguousarray(x[b, half * T:(half + 1) * T, :].T)
        maps.append(m)
    return maps


def l2_in_maps(inputs, x1Ts, kv_fulls):
    common = _common_maps(inputs)
    common["Wq_c"] = np.asarray(inputs["L0_attn_Wq_c"], dtype=np.float32)
    common["bqc_t"] = np.ascontiguousarray(_pad_bias(inputs["L0_attn_bq_c"], 12).T)
    common["Wq_up"] = np.asarray(inputs["L0_attn_Wq_up"], dtype=np.float32)
    bqup = np.asarray(inputs["L0_attn_bq_up"], dtype=np.float32)
    bq_t = np.zeros((128, 64), np.float32)
    for h in range(NH):
        bq_t[:, 2 * h] = bqup[h * QH:h * QH + NOPE]
        bq_t[0:ROPE, 2 * h + 1] = bqup[h * QH + NOPE:(h + 1) * QH]
    common["bqup_t"] = bq_t
    common["Wkv_up"] = np.asarray(inputs["L0_attn_Wkv_up"], dtype=np.float32)
    common["bkvup_t"] = np.ascontiguousarray(_pad_bias(inputs["L0_attn_bkv_up"], 64).T)
    common["Wo"] = np.asarray(inputs["L0_attn_Wo"], dtype=np.float32)
    common["bo_t"] = np.ascontiguousarray(_pad_bias(inputs["L0_attn_bo"], 32).T)
    maps = []
    for c in range(8):
        b = c // 2
        m = dict(common)
        m["xin"] = x1Ts[c]
        m["kv_full"] = kv_fulls[b]
        maps.append(m)
    return maps


def kernel(**inputs):
    maps1 = l1_in_maps(inputs)
    nc1 = build_l1()
    res1 = run_bass_kernel_spmd(nc1, maps1, core_ids=list(range(8)))
    x1Ts = [np.asarray(res1.results[c]["x1T"]) for c in range(8)]
    kv_fulls = [np.concatenate([np.asarray(res1.results[2 * b]["kv_own"]),
                                np.asarray(res1.results[2 * b + 1]["kv_own"])], axis=1)
                for c2 in range(1) for b in range(4)]
    maps2 = l2_in_maps(inputs, x1Ts, kv_fulls)
    nc2 = build_l2()
    res2 = run_bass_kernel_spmd(nc2, maps2, core_ids=list(range(8)))
    out = np.empty((B, S, D), dtype=np.float32)
    for c in range(8):
        b, half = divmod(c, 2)
        out[b, half * T:(half + 1) * T, :] = np.asarray(res2.results[c]["outT"]).T
    return out


# version 06 (final)
# speedup vs baseline: 1.0847x; 1.0847x over previous
import numpy as np
import ml_dtypes
from contextlib import ExitStack
import concourse.bass as bass
import concourse.mybir as mybir
from concourse.bass_utils import run_bass_kernel_spmd

F32 = mybir.dt.float32
BF16 = mybir.dt.bfloat16
I32 = mybir.dt.int32
AF = mybir.ActivationFunctionType
ALU = mybir.AluOpType

B = 4
S = 4096
D = 4096
FF = 11008
T = 2048
NCD = D // 128
NTF = FF // 128
KVL = 512
KVF = 576
QL = 1536
NH = 32
NOPE = 128
ROPE = 64
QH = NOPE + ROPE
VH = 128
SCALE = float(1.0 / np.sqrt(QH))
EPS = 1e-5

ENGS = ("sync", "gpsimd", "vector", "scalar", "tensor")


class Ctr:

    def __init__(self, handle):
        self.h = handle
        self.v = 0

    def inc(self, n=1):
        self.v += n
        return self.v


class Plan:
    def __init__(self, nc, es):
        self.nc = nc
        self.es = es
        self.ops = {e: [] for e in ENGS}
        self.dsems = []
        self._uid = [0]

    def dsem(self, eng, name):
        self._uid[0] += 1
        c = Ctr(self.es.enter_context(self.nc.semaphore(f"{name}_{self._uid[0]}")))
        self.dsems.append((eng, c))
        return c

    def emit(self, eng, fn):
        self.ops[eng].append(fn)

    def wait(self, eng, ctr, val):
        if val <= 0:
            return
        self.emit(eng, lambda e, h=ctr.h, v=val: e.wait_ge(h, v))

    def run(self, block):
        for eng in ENGS:
            ops = self.ops[eng]

            def body(e, ops=ops):
                for f in ops:
                    f(e)

            getattr(block, eng)(body)


def dma(plan, eng, dsem, out, in_, **kw):
    v = dsem.inc(16)
    plan.emit(eng, lambda e, o=out, i=in_, h=dsem.h, kw=kw:
              e.dma_start(out=o, in_=i, **kw).then_inc(h, 16))
    return v


def op(plan, eng, ctr, fn):
    v = ctr.inc(1)
    plan.emit(eng, lambda e, fn=fn, h=ctr.h: fn(e).then_inc(h, 1))
    return v


def opq(plan, eng, fn):
    plan.emit(eng, fn)


def barrier(plan, bar, compute=()):
    for eng, c in plan.dsems:
        plan.wait(eng, c, c.v)
    for eng, c in compute:
        plan.wait(eng, c, c.v)
    v = bar.inc(len(ENGS))
    for eng in ENGS:
        plan.emit(eng, lambda e, h=bar.h: e.nop().then_inc(h, 1))
    for eng in ENGS:
        plan.emit(eng, lambda e, h=bar.h, v=v: e.wait_ge(h, v))
    return v



def emit_l2_attention(nc, plan, es, sbL, pb, sV, sS, sT, sB, CMP, id_s, ones_s,
                      bqcT_s, bqupT_s, bkvupT_s, boT_s,
                      x1T, x2T, kv16, wqcTh, wqupTh, wkvupTh, wo16, xn2h,
                      mkdsem, dsConst, v_const,
                      dsCast, cast_done, dma, barrier):
    att_es = ExitStack()
    woT_s = att_es.enter_context(sbL("woT", [128, D], BF16))
    o_sum = att_es.enter_context(sbL("o_sum", [128, 16 * VH], F32))
    osT = att_es.enter_context(sbL("osT", [128, 16 * VH], BF16))
    boc_s = att_es.enter_context(sbL("boc", [128, 32], F32))
    sqbup_s = att_es.enter_context(sbL("sqbup", [128, 64], F32))
    kmx_s = att_es.enter_context(sbL("kmx", [1, 16], F32))
    qn2r_s = att_es.enter_context(sbL("qn2r", [1, T], F32))
    srow_s = att_es.enter_context(sbL("srow", [1, T], F32))
    bvs_s = att_es.enter_context(sbL("bvs", [128, 1], F32))
    bvs16_s = att_es.enter_context(sbL("bvs16", [128, 1], BF16))
    rc_s = att_es.enter_context(sbL("rc", [128, 4], F32))

    head_es = ExitStack()
    qc_s = head_es.enter_context(sbL("qc_s", [128, 12 * T], BF16))

    dsWq = [mkdsem("sync", f"dsWq{k}") for k in range(2)]
    dsXn2 = [mkdsem("sync", f"dsXn2{k}") for k in range(2)]
    with sbL("wqf0", [128, NCD * 128], BF16) as wqf0, \
         sbL("wqf1", [128, NCD * 128], BF16) as wqf1, \
         sbL("xnc0", [128, T], BF16) as xnc0, \
         sbL("xnc1", [128, T], BF16) as xnc1:
        wqf = [wqf0, wqf1]
        xnc = [xnc0, xnc1]
        use_wq = [0, 0]
        use_xn = [0, 0]
        qc_ev = [0, 0]
        for ft in range(12):
            sl = ft % 2
            bk0 = 4 * sl
            plan.wait("sync", sT, use_wq[sl])
            src = wqcTh.rearrange("p (c f) -> p c f", f=QL)[:, :, ft * 128:(ft + 1) * 128]
            wv = dma(plan, "sync", dsWq[sl],
                     wqf[sl][:].rearrange("p (c j) -> p c j", j=128), src)
            plan.wait("tensor", dsWq[sl], wv)
            plan.wait("tensor", sV, qc_ev[sl])
            for c in range(NCD):
                xsl = c % 2
                plan.wait("sync", sT, use_xn[xsl])
                xlv = dma(plan, "sync", dsXn2[xsl], xnc[xsl][:],
                          xn2h[c * 128:(c + 1) * 128, :])
                plan.wait("tensor", dsXn2[xsl], xlv)
                for nt in range(4):
                    fn = lambda e, c=c, nt=nt, sl=sl, bk0=bk0, xsl=xsl: e.matmul(
                        pb[bk0 + nt][:, :], wqf[sl][:, c * 128:(c + 1) * 128],
                        xnc[xsl][:, nt * 512:(nt + 1) * 512],
                        start=(c == 0), stop=(c == NCD - 1))
                    if nt == 3:
                        use_xn[xsl] = mmv = op(plan, "tensor", sT, fn)
                    else:
                        opq(plan, "tensor", fn)
            use_wq[sl] = mmv
            plan.wait("vector", sT, mmv)
            for nt in range(4):
                fn = lambda e, ft=ft, nt=nt, bk0=bk0: e.tensor_scalar_add(
                    qc_s[:, ft * T + nt * 512:ft * T + (nt + 1) * 512],
                    pb[bk0 + nt][:, :], bqcT_s[:, ft:ft + 1])
                if nt == 3:
                    qc_ev[sl] = op(plan, "vector", sV, fn)
                else:
                    opq(plan, "vector", fn)
    barrier(plan, sB, compute=CMP)

    kvn_s = [head_es.enter_context(sbL(f"kvn{l}", [128, S], BF16)) for l in range(4)]
    krop_s = head_es.enter_context(sbL("krop", [65, S], BF16))
    krop2_s = head_es.enter_context(sbL("krop2", [64, S], BF16))

    dsKvL = mkdsem("sync", "dsKvL")
    plan.wait("sync", dsCast["a"], cast_done["a"])
    for l in range(4):
        dma(plan, "sync", dsKvL, kvn_s[l][:], kv16[l * 128:(l + 1) * 128, :])
    kv_lv = dma(plan, "sync", dsKvL, krop_s[0:64, :], kv16[KVL:KVF, :])

    dsWoT = mkdsem("sync", "dsWoT")
    wot_lv = dma(plan, "sync", dsWoT, woT_s[:], wo16, transpose=True)

    plan.wait("vector", dsKvL, kv_lv)
    mz = op(plan, "vector", sV, lambda e: e.memset(krop_s[64:65, :], 1.0))
    op(plan, "vector", sV, lambda e: e.memset(o_sum[:], 0.0))
    plan.wait("vector", dsConst, v_const)
    sqb = op(plan, "vector", sV, lambda e: e.tensor_scalar_mul(sqbup_s[:], bqupT_s[:], SCALE))
    vcols = bkvupT_s[:].rearrange("p (h two) -> p h two", two=2)[:, :, 1:2].rearrange(
        "p h one -> p (h one)")
    bv = op(plan, "vector", sV, lambda e: e.reduce_sum(
        bvs_s[:], vcols, axis=mybir.AxisListType.X))
    plan.wait("vector", sV, bv)
    bv16 = op(plan, "vector", sV, lambda e: e.tensor_copy(bvs16_s[:], bvs_s[:]))

    plan.wait("scalar", sV, mz)
    kr2 = op(plan, "scalar", sS, lambda e: e.activation(krop2_s[:], krop_s[0:64, :], AF.Square))

    plan.wait("tensor", dsWoT, wot_lv)
    plan.wait("tensor", sV, bv16)
    for Ht in range(32):
        fn = lambda e, Ht=Ht: e.matmul(
            pb[6][:, Ht:Ht + 1], woT_s[:, Ht * 128:(Ht + 1) * 128], bvs16_s[:],
            start=True, stop=True)
        if Ht == 31:
            bocmm = op(plan, "tensor", sT, fn)
        else:
            opq(plan, "tensor", fn)
    plan.wait("vector", sT, bocmm)
    boc_v = op(plan, "vector", sV, lambda e: e.tensor_add(boc_s[:], pb[6][:, 0:32], boT_s[:]))

    qwh_s = head_es.enter_context(sbL("qwh", [128, 12 * QH], BF16))
    kvwh_s = head_es.enter_context(sbL("kvwh", [128, 4 * 256], BF16))
    k_s = head_es.enter_context(sbL("k_h", [128, S], BF16))
    sq_s = head_es.enter_context(sbL("sqscr", [128, S], BF16))
    v_s = head_es.enter_context(sbL("v_h", [128, 32 * 130], BF16))
    qn_s = head_es.enter_context(sbL("qn_h", [128, T], BF16))
    qr_s = head_es.enter_context(sbL("qr_h", [65, T], BF16))
    pr_s = [head_es.enter_context(sbL(f"pr{k}", [128, 512], BF16)) for k in range(4)]

    vz = op(plan, "vector", sV, lambda e: e.memset(
        v_s[:].rearrange("p (kc j) -> p kc j", j=130)[:, :, 128:130], 1.0))

    dsQwh = mkdsem("sync", "dsQwh")
    dsKvwh = mkdsem("sync", "dsKvwh")

    qwh_use = 0
    kvwh_use = 0
    kgen_ev = 0
    vgen_ev = 0
    qgen_ev = 0
    osum_ev = 0
    b6_ev = boc_v
    exp_done = [0, 0, 0, 0]
    attnv_done = [0, 0, 0, 0]
    norm_done = 0

    plan.wait("tensor", sV, vz)
    for h in range(NH):
        plan.wait("sync", sT, qwh_use)
        qsrc = wqupTh.rearrange("p (l f) -> p l f", f=NH * QH)[:, :, h * QH:(h + 1) * QH]
        qw_lv = dma(plan, "sync", dsQwh, qwh_s[:].rearrange("p (l j) -> p l j", j=QH), qsrc)
        plan.wait("sync", sT, kvwh_use)
        ksrc = wkvupTh.rearrange("p (l f) -> p l f", f=NH * 256)[:, :, h * 256:(h + 1) * 256]
        kv_lv2 = dma(plan, "sync", dsKvwh, kvwh_s[:].rearrange("p (l j) -> p l j", j=256), ksrc)

        plan.wait("tensor", dsKvwh, kv_lv2)
        plan.wait("tensor", dsKvL, kv_lv)
        plan.wait("tensor", sV, kgen_ev)
        plan.wait("tensor", sV, osum_ev)
        kev = 0
        for nt in range(8):
            bk = 4 + (nt % 2)
            if nt >= 2:
                plan.wait("tensor", sV, kev)
            for l in range(4):
                fn = lambda e, nt=nt, l=l, bk=bk: e.matmul(
                    pb[bk][:, :], kvwh_s[:, l * 256:l * 256 + 128],
                    kvn_s[l][:, nt * 512:(nt + 1) * 512],
                    start=(l == 0), stop=(l == 3))
                if l == 3:
                    kmm = op(plan, "tensor", sT, fn)
                else:
                    opq(plan, "tensor", fn)
            plan.wait("vector", sT, kmm)
            kev = op(plan, "vector", sV, lambda e, nt=nt, bk=bk, h=h: e.tensor_scalar_add(
                k_s[:, nt * 512:(nt + 1) * 512], pb[bk][:, :],
                bkvupT_s[:, 2 * h:2 * h + 1]))
        kgen_ev = kev

        plan.wait("scalar", sV, kgen_ev)
        plan.wait("scalar", sV, norm_done)
        ksq = op(plan, "scalar", sS, lambda e: e.activation(sq_s[:], k_s[:], AF.Square))
        plan.wait("tensor", sV, vgen_ev)
        vev = 0
        for kc in range(32):
            off = 256 * (kc % 2)
            if kc >= 2:
                plan.wait("tensor", sV, vev)
            for l in range(4):
                fn = lambda e, kc=kc, l=l, off=off: e.matmul(
                    pb[7][:, off:off + 128], kvn_s[l][:, kc * 128:(kc + 1) * 128],
                    kvwh_s[:, l * 256 + 128:l * 256 + 256],
                    start=(l == 0), stop=(l == 3))
                if l == 3:
                    vmm = op(plan, "tensor", sT, fn)
                else:
                    opq(plan, "tensor", fn)
            plan.wait("vector", sT, vmm)
            vev = op(plan, "vector", sV, lambda e, kc=kc, off=off: e.tensor_copy(
                v_s[:, kc * 130:kc * 130 + 128], pb[7][:, off:off + 128]))
        vgen_ev = vev

        plan.wait("tensor", sS, ksq)
        plan.wait("tensor", sV, b6_ev)
        kn_ev = 0
        for nt in range(8):
            if nt > 0:
                plan.wait("tensor", sV, kn_ev)
            opq(plan, "tensor", lambda e, nt=nt: e.matmul(
                pb[6][0:1, :], ones_s[:, 0:1], sq_s[:, nt * 512:(nt + 1) * 512],
                start=True, stop=False))
            knm = op(plan, "tensor", sT, lambda e, nt=nt: e.matmul(
                pb[6][0:1, :], ones_s[0:64, 0:1], krop2_s[:, nt * 512:(nt + 1) * 512],
                start=False, stop=True))
            plan.wait("vector", sT, knm)
            kn_ev = op(plan, "vector", sV, lambda e, nt=nt: e.reduce_max(
                kmx_s[0:1, nt:nt + 1], pb[6][0:1, :], axis=mybir.AxisListType.X))
        plan.wait("vector", sV, kn_ev)
        kmax = op(plan, "vector", sV, lambda e: e.reduce_max(
            kmx_s[0:1, 8:9], kmx_s[0:1, 0:8], axis=mybir.AxisListType.X))
        b6_ev = kn_ev

        plan.wait("tensor", dsQwh, qw_lv)
        plan.wait("tensor", sV, qgen_ev)
        plan.wait("tensor", sV, kmax)
        qev = 0
        qn_last = 0
        for nt in range(4):
            bk = 4 + (nt % 2)
            if nt >= 1:
                plan.wait("tensor", sV, qn_last)
            for l in range(12):
                fn = lambda e, nt=nt, l=l, bk=bk: e.matmul(
                    pb[bk][:, :], qwh_s[:, l * QH:l * QH + NOPE],
                    qc_s[:, l * T + nt * 512:l * T + (nt + 1) * 512],
                    start=(l == 0), stop=(l == 11))
                if l == 11:
                    qmm = op(plan, "tensor", sT, fn)
                else:
                    opq(plan, "tensor", fn)
            for l in range(12):
                fn = lambda e, nt=nt, l=l: e.matmul(
                    pb[6][0:64, :], qwh_s[:, l * QH + NOPE:(l + 1) * QH],
                    qc_s[:, l * T + nt * 512:l * T + (nt + 1) * 512],
                    start=(l == 0), stop=(l == 11))
                if l == 11:
                    qrmm = op(plan, "tensor", sT, fn)
                else:
                    opq(plan, "tensor", fn)
            plan.wait("vector", sT, qrmm)
            plan.wait("vector", sV, sqb)
            opq(plan, "vector", lambda e, nt=nt, bk=bk, h=h: e.tensor_scalar(
                qn_s[:, nt * 512:(nt + 1) * 512], pb[bk][:, :],
                SCALE, sqbup_s[:, 2 * h:2 * h + 1],
                ALU.mult, ALU.add))
            qn_last = op(plan, "vector", sV, lambda e, nt=nt, bk=bk, h=h: e.tensor_scalar(
                qr_s[0:64, nt * 512:(nt + 1) * 512], pb[6][0:64, :],
                SCALE, sqbup_s[0:64, 2 * h + 1:2 * h + 2],
                ALU.mult, ALU.add))
            qev = qn_last
        qgen_ev = qev
        b6_ev = qev

        plan.wait("scalar", sV, qgen_ev)
        qsq = op(plan, "scalar", sS, lambda e: e.activation(
            sq_s[:, 0:T], qn_s[:], AF.Square))
        qsq2 = op(plan, "scalar", sS, lambda e: e.activation(
            sq_s[0:64, T:2 * T], qr_s[0:64, :], AF.Square))
        plan.wait("tensor", sS, qsq2)
        plan.wait("tensor", sV, b6_ev)
        qn_ev = 0
        for nt in range(4):
            if nt > 0:
                plan.wait("tensor", sV, qn_ev)
            opq(plan, "tensor", lambda e, nt=nt: e.matmul(
                pb[6][0:1, :], ones_s[:, 0:1], sq_s[:, nt * 512:(nt + 1) * 512],
                start=True, stop=False))
            qnm = op(plan, "tensor", sT, lambda e, nt=nt: e.matmul(
                pb[6][0:1, :], ones_s[0:64, 0:1],
                sq_s[0:64, T + nt * 512:T + (nt + 1) * 512],
                start=False, stop=True))
            plan.wait("vector", sT, qnm)
            qn_ev = op(plan, "vector", sV, lambda e, nt=nt: e.tensor_copy(
                qn2r_s[0:1, nt * 512:(nt + 1) * 512], pb[6][0:1, :]))
        b6_ev = qn_ev
        plan.wait("vector", sV, qn_ev)
        pr = op(plan, "vector", sV, lambda e: e.tensor_scalar_mul(
            srow_s[:], qn2r_s[:], kmx_s[0:1, 8:9]))
        plan.wait("scalar", sV, pr)
        sq2 = op(plan, "scalar", sS, lambda e: e.activation(srow_s[:], srow_s[:], AF.Sqrt))
        plan.wait("vector", sS, sq2)
        norm_done = op(plan, "vector", sV, lambda e: e.tensor_scalar_mul(
            qr_s[64:65, :], srow_s[:], -1.0))

        SC_BANKS = (0, 1, 6, 7)
        plan.wait("tensor", sV, norm_done)
        for tq in range(4):
            def emit_scores(kc, tq=tq):
                sl = kc % 4
                bk = SC_BANKS[sl]
                plan.wait("tensor", sS, exp_done[sl])
                opq(plan, "tensor", lambda e, kc=kc, tq=tq, bk=bk: e.matmul(
                    pb[bk][:, :], k_s[:, kc * 128:(kc + 1) * 128],
                    qn_s[:, tq * 512:(tq + 1) * 512], start=True, stop=False))
                scv = op(plan, "tensor", sT, lambda e, kc=kc, tq=tq, bk=bk: e.matmul(
                    pb[bk][:, :], krop_s[:, kc * 128:(kc + 1) * 128],
                    qr_s[:, tq * 512:(tq + 1) * 512], start=False, stop=True))
                plan.wait("scalar", sT, scv)
                plan.wait("scalar", sT, attnv_done[sl])
                exp_done[sl] = op(plan, "scalar", sS, lambda e, sl=sl, bk=bk:
                                  e.activation(pr_s[sl][:], pb[bk][:, :], AF.Exp))

            def emit_attnv(kc):
                sl = kc % 4
                plan.wait("tensor", sS, exp_done[sl])
                if kc == 0:
                    plan.wait("tensor", sV, osum_ev)
                for qt in range(4):
                    bk = 2 + qt
                    fn = lambda e, kc=kc, qt=qt, bk=bk, sl=sl: e.matmul(
                        pb[bk][:, 0:129],
                        pr_s[sl][:, qt * 128:(qt + 1) * 128],
                        v_s[:, kc * 130:kc * 130 + 129],
                        start=(kc == 0), stop=(kc == 31))
                    if qt == 3:
                        attnv_done[sl] = op(plan, "tensor", sT, fn)
                    else:
                        opq(plan, "tensor", fn)

            for kc in range(32):
                emit_scores(kc)
                if kc >= 3:
                    emit_attnv(kc - 3)
            for kc in range(29, 32):
                emit_attnv(kc)
            plan.wait("vector", sT, attnv_done[3])
            for qt in range(4):
                bk = 2 + qt
                rcv = op(plan, "vector", sV, lambda e, qt=qt, bk=bk: e.reciprocal(
                    rc_s[:, qt:qt + 1], pb[bk][:, 128:129]))
                plan.wait("vector", sV, rcv)
                g = tq * 4 + qt
                osum_ev = op(plan, "vector", sV, lambda e, qt=qt, bk=bk, g=g:
                             e.scalar_tensor_tensor(
                                 o_sum[:, g * VH:(g + 1) * VH], pb[bk][:, 0:VH],
                                 rc_s[:, qt:qt + 1], o_sum[:, g * VH:(g + 1) * VH],
                                 ALU.mult, ALU.add))
        qwh_use = qrmm
        kvwh_use = vmm

    plan.wait("tensor", sV, osum_ev)
    plan.wait("tensor", sS, max(exp_done))
    ot_ev = 0
    for g in range(16):
        bk = g % 4
        if g >= 4:
            plan.wait("tensor", sV, ot_ev)
        tv = op(plan, "tensor", sT, lambda e, g=g, bk=bk: e.transpose(
            pb[bk][:, 0:128], o_sum[:, g * VH:(g + 1) * VH], id_s[:]))
        plan.wait("vector", sT, tv)
        ot_ev = op(plan, "vector", sV, lambda e, g=g, bk=bk: e.tensor_copy(
            osT[:, g * VH:(g + 1) * VH], pb[bk][:, 0:128]))

    plan.wait("scalar", sT, max(attnv_done))
    head_es.close()

    dsX1r = [mkdsem("scalar", f"dsX1r{k}") for k in range(2)]
    dsX2 = [mkdsem("gpsimd", f"dsX2{k}") for k in range(2)]
    with sbL("opx0", [128, T], F32) as opx0, \
         sbL("opx1", [128, T], F32) as opx1, \
         sbL("opo0", [128, T], F32) as opo0, \
         sbL("opo1", [128, T], F32) as opo1:
        opx = [opx0, opx1]
        opo = [opo0, opo1]
        x2st = [0, 0]
        opev = [0, 0]
        plan.wait("vector", sV, boc_v)
        plan.wait("tensor", sV, ot_ev)
        for Ht in range(32):
            sl = Ht % 2
            bk0 = 4 * sl
            plan.wait("scalar", dsX2[sl], x2st[sl])
            plan.wait("scalar", sV, opev[sl])
            xv = dma(plan, "scalar", dsX1r[sl], opx[sl][:],
                     x1T[Ht * 128:(Ht + 1) * 128, :])
            plan.wait("tensor", sV, opev[sl])
            for nt in range(4):
                fn = lambda e, Ht=Ht, nt=nt, bk0=bk0: e.matmul(
                    pb[bk0 + nt][:, :], woT_s[:, Ht * 128:(Ht + 1) * 128],
                    osT[:, nt * 512:(nt + 1) * 512], start=True, stop=True)
                if nt == 3:
                    omm = op(plan, "tensor", sT, fn)
                else:
                    opq(plan, "tensor", fn)
            plan.wait("vector", sT, omm)
            plan.wait("vector", dsX1r[sl], xv)
            for nt in range(4):
                fn = lambda e, Ht=Ht, nt=nt, sl=sl, bk0=bk0: e.scalar_tensor_tensor(
                    opo[sl][:, nt * 512:(nt + 1) * 512], pb[bk0 + nt][:, :],
                    boc_s[:, Ht:Ht + 1], opx[sl][:, nt * 512:(nt + 1) * 512],
                    ALU.add, ALU.add)
                if nt == 3:
                    opev[sl] = op(plan, "vector", sV, fn)
                else:
                    opq(plan, "vector", fn)
            plan.wait("gpsimd", sV, opev[sl])
            x2st[sl] = dma(plan, "gpsimd", dsX2[sl],
                           x2T[Ht * 128:(Ht + 1) * 128, :], opo[sl][:])

    barrier(plan, sB, compute=CMP)
    att_es.close()

def build_fused(debug=False):
    nc = bass.Bass("TRN2", target_bir_lowering=False)

    xoT = nc.dram_tensor("xoT", [D, T], F32, kind="ExternalInput").ap()
    xpT = nc.dram_tensor("xpT", [D, T], F32, kind="ExternalInput").ap()
    qw_g = nc.dram_tensor("qw_gateT", [FF, D], I32, kind="ExternalInput").ap()
    z_g = nc.dram_tensor("z_gate", [32, FF], I32, kind="ExternalInput").ap()
    s_g = nc.dram_tensor("s_gate", [32, FF], F32, kind="ExternalInput").ap()
    qw_u = nc.dram_tensor("qw_upT", [FF, D], I32, kind="ExternalInput").ap()
    z_u = nc.dram_tensor("z_up", [32, FF], I32, kind="ExternalInput").ap()
    s_u = nc.dram_tensor("s_up", [32, FF], F32, kind="ExternalInput").ap()
    qw_d = nc.dram_tensor("qw_downT", [D, FF], I32, kind="ExternalInput").ap()
    z_d = nc.dram_tensor("z_down", [86, D], I32, kind="ExternalInput").ap()
    s_d = nc.dram_tensor("s_down", [86, D], F32, kind="ExternalInput").ap()
    id_f = nc.dram_tensor("id_f", [128, 128], F32, kind="ExternalInput").ap()
    ones_b = nc.dram_tensor("ones_b", [128, 128], BF16, kind="ExternalInput").ap()
    epsb = nc.dram_tensor("epsb", [128, 1], F32, kind="ExternalInput").ap()

    wkvc = nc.dram_tensor("Wkv_c", [640, D], F32, kind="ExternalInput").ap()
    bkvc_t = nc.dram_tensor("bkvc_t", [128, 5], F32, kind="ExternalInput").ap()
    wqc = nc.dram_tensor("Wq_c", [QL, D], F32, kind="ExternalInput").ap()
    bqc_t = nc.dram_tensor("bqc_t", [128, 12], F32, kind="ExternalInput").ap()
    wqup = nc.dram_tensor("Wq_up", [NH * QH, QL], F32, kind="ExternalInput").ap()
    bqup_t = nc.dram_tensor("bqup_t", [128, 64], F32, kind="ExternalInput").ap()
    wkvup = nc.dram_tensor("Wkv_up", [NH * 256, KVL], F32, kind="ExternalInput").ap()
    bkvup_t = nc.dram_tensor("bkvup_t", [128, 64], F32, kind="ExternalInput").ap()
    wo = nc.dram_tensor("Wo", [D, VH], F32, kind="ExternalInput").ap()
    bo_t = nc.dram_tensor("bo_t", [128, 32], F32, kind="ExternalInput").ap()
    outT = nc.dram_tensor("outT", [D, T], F32, kind="ExternalOutput").ap()
    x1oT = nc.dram_tensor("x1oT", [D, T], F32, kind="Internal").ap()
    x1pT = nc.dram_tensor("x1pT", [D, T], F32, kind="Internal").ap()
    x2T = nc.dram_tensor("x2T", [D, T], F32, kind="Internal").ap()
    xn2h = nc.dram_tensor("xn2h", [D, T], BF16, kind="Internal").ap()
    kv16 = nc.dram_tensor("kv16", [KVF, S], BF16, kind="Internal").ap()

    dTg = nc.dram_tensor("dTg", [FF, D], BF16, kind="Internal").ap()
    dTu = nc.dram_tensor("dTu", [FF, D], BF16, kind="Internal").ap()
    dTd = nc.dram_tensor("dTd", [D, FF], BF16, kind="Internal").ap()
    wgB = nc.dram_tensor("wgB", [43, D, 256], BF16, kind="Internal").ap()
    wuB = nc.dram_tensor("wuB", [43, D, 256], BF16, kind="Internal").ap()
    wdB = nc.dram_tensor("wdB", [16, FF, 256], BF16, kind="Internal").ap()
    hact = nc.dram_tensor("hact", [FF, T], BF16, kind="Internal").ap()

    es = ExitStack()
    plan = Plan(nc, es)

    sV = Ctr(es.enter_context(nc.semaphore("sV")))
    sS = Ctr(es.enter_context(nc.semaphore("sS")))
    sT = Ctr(es.enter_context(nc.semaphore("sT")))
    sB = Ctr(es.enter_context(nc.semaphore("sB")))
    CMP = (("vector", sV), ("scalar", sS), ("tensor", sT))

    pb = [es.enter_context(nc.psum_tensor(f"pb{i}", [128, 512], F32)) for i in range(8)]

    _uid = [0]

    def sbL(name, shape, dt):
        _uid[0] += 1
        return nc.sbuf_tensor(f"{name}_{_uid[0]}", shape, dt, side="left")

    def sbR(name, shape, dt):
        _uid[0] += 1
        return nc.sbuf_tensor(f"{name}_{_uid[0]}", shape, dt, side="right")

    id_s = es.enter_context(sbL("id_s", [128, 128], F32))
    ones_s = es.enter_context(sbL("ones_s", [128, 128], BF16))
    epsb_s = es.enter_context(sbL("epsb_s", [128, 1], F32))

    dsConst = plan.dsem("sync", "dsConst")
    dma(plan, "sync", dsConst, id_s[:], id_f)
    dma(plan, "sync", dsConst, ones_s[:], ones_b)
    dma(plan, "sync", dsConst, epsb_s[:], epsb)
    bkvcT_s = es.enter_context(sbL("bkvcT_s", [128, 5], F32))
    dma(plan, "sync", dsConst, bkvcT_s[:], bkvc_t)
    bqcT_s = es.enter_context(sbL("bqcT_s", [128, 12], F32))
    bqupT_s = es.enter_context(sbL("bqupT_s", [128, 64], F32))
    bkvupT_s = es.enter_context(sbL("bkvupT_s", [128, 64], F32))
    boT_s = es.enter_context(sbL("boT_s", [128, 32], F32))
    dma(plan, "sync", dsConst, bqcT_s[:], bqc_t)
    dma(plan, "sync", dsConst, bqupT_s[:], bqup_t)
    dma(plan, "sync", dsConst, bkvupT_s[:], bkvup_t)
    v_const = dma(plan, "sync", dsConst, boT_s[:], bo_t)

    dsCast = {"a": plan.dsem("gpsimd", "dsCast_a")}
    cast_done = {}
    wkvc16 = nc.dram_tensor("wkvc16", [640, D], BF16, kind="Internal").ap()
    wkvcTh = nc.dram_tensor("wkvcTh", [128, NCD * 640], BF16, kind="Internal").ap()
    wqc16 = nc.dram_tensor("wqc16", [QL, D], BF16, kind="Internal").ap()
    wqup16 = nc.dram_tensor("wqup16", [NH * QH, QL], BF16, kind="Internal").ap()
    wkvup16 = nc.dram_tensor("wkvup16", [NH * 256, KVL], BF16, kind="Internal").ap()
    wo16 = nc.dram_tensor("wo16", [D, VH], BF16, kind="Internal").ap()
    wqcTh = nc.dram_tensor("wqcTh", [128, NCD * QL], BF16, kind="Internal").ap()
    wqupTh = nc.dram_tensor("wqupTh", [128, 12 * NH * QH], BF16, kind="Internal").ap()
    wkvupTh = nc.dram_tensor("wkvupTh", [128, 4 * NH * 256], BF16, kind="Internal").ap()
    dma(plan, "gpsimd", dsCast["a"], wkvc16, wkvc)
    dma(plan, "gpsimd", dsCast["a"], wqc16, wqc)
    dma(plan, "gpsimd", dsCast["a"], wqup16, wqup)
    dma(plan, "gpsimd", dsCast["a"], wkvup16, wkvup)
    cast_done["a"] = dma(plan, "gpsimd", dsCast["a"], wo16, wo)

    zs_es = ExitStack()
    zgT = zs_es.enter_context(sbL("zgT", [128, 86 * 32], F32))
    sgT = zs_es.enter_context(sbL("sgT", [128, 86 * 32], F32))
    zuT = zs_es.enter_context(sbL("zuT", [128, 86 * 32], F32))
    suT = zs_es.enter_context(sbL("suT", [128, 86 * 32], F32))
    zdT = zs_es.enter_context(sbL("zdT", [128, 32 * 86], F32))
    sdT = zs_es.enter_context(sbL("sdT", [128, 32 * 86], F32))

    zsraw_es = ExitStack()
    zraw_sh = zsraw_es.enter_context(sbL("zs_zraw", [86, FF], F32))
    sraw_sh = zsraw_es.enter_context(sbL("zs_sraw", [86, FF], F32))
    dsZraw = plan.dsem("gpsimd", "dsZraw")
    dsSraw = plan.dsem("sync", "dsSraw")
    zs_state = {"last_pe": 0, "prev_ev": [0, 0, 0, 0]}

    def zs_prep(zdram, sdram, zT_s, sT_s, rows, ncols_t, tile_w):
        plan.wait("gpsimd", sT, zs_state["last_pe"])
        plan.wait("sync", sT, zs_state["last_pe"])
        vz = dma(plan, "gpsimd", dsZraw, zraw_sh[0:rows, 0:ncols_t * 128], zdram)
        vs = dma(plan, "sync", dsSraw, sraw_sh[0:rows, 0:ncols_t * 128], sdram)
        plan.wait("tensor", dsZraw, vz)
        plan.wait("tensor", dsSraw, vs)
        plan.wait("tensor", dsConst, v_const)
        prev_ev = zs_state["prev_ev"]
        for i in range(ncols_t):
            bk = 4 + (i % 4)
            plan.wait("tensor", sV, prev_ev[i % 4])
            opq(plan, "tensor", lambda e, i=i, bk=bk: e.transpose(
                pb[bk][:, 0:tile_w], zraw_sh[0:rows, i * 128:(i + 1) * 128],
                id_s[0:rows, 0:rows]))
            tv = op(plan, "tensor", sT, lambda e, i=i, bk=bk: e.transpose(
                pb[bk][:, 128:128 + tile_w], sraw_sh[0:rows, i * 128:(i + 1) * 128],
                id_s[0:rows, 0:rows]))
            zs_state["last_pe"] = tv
            plan.wait("vector", sT, tv)
            opq(plan, "vector", lambda e, i=i, bk=bk: e.tensor_copy(
                zT_s[:, i * tile_w:(i + 1) * tile_w], pb[bk][:, 0:tile_w]))
            prev_ev[i % 4] = op(plan, "vector", sV, lambda e, i=i, bk=bk: e.tensor_copy(
                sT_s[:, i * tile_w:(i + 1) * tile_w], pb[bk][:, 128:128 + tile_w]))

    zs_prep(z_g, s_g, zgT, sgT, 32, 86, 32)
    zs_prep(z_u, s_u, zuT, suT, 32, 86, 32)
    zs_prep(z_d, s_d, zdT, sdT, 86, 32, 86)
    zsraw_es.close()

    barrier(plan, sB, compute=CMP)

    dsXb = [plan.dsem("sync", f"dsXb{k}") for k in range(3)]

    def emit_stats(x_src, mb_out, invb_out):
        with sbL("st_scr", [128, T], F32) as scr, \
             sbL("st_xb0", [128, T], F32) as xb0, \
             sbL("st_xb1", [128, T], F32) as xb1, \
             sbL("st_xb2", [128, T], F32) as xb2, \
             sbL("st_x16a", [128, T], BF16) as x16a, \
             sbL("st_x16b", [128, T], BF16) as x16b, \
             sbL("st_xsqa", [128, T], BF16) as xsqa, \
             sbL("st_xsqb", [128, T], BF16) as xsqb:
            xb = [xb0, xb1, xb2]
            x16 = [x16a, x16b]
            xsq = [xsqa, xsqb]
            use_pe = [0, 0, 0]
            use2 = [0, 0]
            for c in range(NCD):
                sl = c % 3
                s2 = c % 2
                plan.wait("sync", sT, use_pe[sl])
                lv = dma(plan, "sync", dsXb[sl], xb[sl][:], x_src[c * 128:(c + 1) * 128, :])
                plan.wait("vector", dsXb[sl], lv)
                plan.wait("vector", sT, use2[s2])
                cv = op(plan, "vector", sV, lambda e, sl=sl, s2=s2: e.tensor_copy(x16[s2][:], xb[sl][:]))
                plan.wait("scalar", dsXb[sl], lv)
                plan.wait("scalar", sT, use2[s2])
                qv = op(plan, "scalar", sS, lambda e, sl=sl, s2=s2: e.activation(
                    xsq[s2][:], xb[sl][:], AF.Square))
                plan.wait("tensor", sV, cv)
                plan.wait("tensor", sS, qv)
                for nt in range(4):
                    opq(plan, "tensor", lambda e, nt=nt, s2=s2, c=c: e.matmul(
                        pb[nt][:, :], ones_s[:], x16[s2][:, nt * 512:(nt + 1) * 512],
                        start=(c == 0), stop=(c == NCD - 1)))
                for nt in range(4):
                    fn = lambda e, nt=nt, s2=s2, c=c: e.matmul(
                        pb[4 + nt][:, :], ones_s[:], xsq[s2][:, nt * 512:(nt + 1) * 512],
                        start=(c == 0), stop=(c == NCD - 1))
                    if nt == 3:
                        use_pe[sl] = use2[s2] = op(plan, "tensor", sT, fn)
                    else:
                        opq(plan, "tensor", fn)
            pe_done = use_pe[(NCD - 1) % 3]
            plan.wait("scalar", sT, pe_done)
            for nt in range(4):
                opq(plan, "scalar", lambda e, nt=nt: e.activation(
                    mb_out[:, nt * 512:(nt + 1) * 512], pb[nt][:, :], AF.Copy, scale=1.0 / D))
            for nt in range(4):
                ev = op(plan, "scalar", sS, lambda e, nt=nt: e.activation(
                    scr[:, nt * 512:(nt + 1) * 512], pb[4 + nt][:, :], AF.Copy, scale=1.0 / D))
            plan.wait("vector", sS, ev)
            mv = op(plan, "vector", sV, lambda e: e.tensor_mul(invb_out[:], mb_out[:], mb_out[:]))
            plan.wait("vector", sV, mv)
            vv = op(plan, "vector", sV, lambda e: e.tensor_sub(scr[:], scr[:], invb_out[:]))
            plan.wait("scalar", sV, vv)
            svv = op(plan, "scalar", sS, lambda e: e.activation(
                scr[:], scr[:], AF.Sqrt, bias=epsb_s[:, 0:1]))
            plan.wait("vector", sS, svv)
            return op(plan, "vector", sV, lambda e: e.reciprocal(invb_out[:], scr[:]))

    dq_es = ExitStack()
    dqi = [dq_es.enter_context(sbL(f"dqi{k}", [128, FF // 2], I32)) for k in range(2)]
    dqo = [dq_es.enter_context(sbL(f"dqo{k}", [128, FF // 2], BF16)) for k in range(2)]
    dsDqi = [plan.dsem("scalar", f"dsDqi{k}") for k in range(2)]
    dsDqo = [plan.dsem("gpsimd", f"dsDqo{k}") for k in range(2)]
    dq_state = {"use_v": [0, 0], "use_st": [0, 0], "hook": None}

    def dequant(qwT, dT_out, zT_s, sT_s, nslices, width, ngroups):
        use_v = dq_state["use_v"]
        use_st = dq_state["use_st"]
        w2 = width // 2
        g2 = ngroups // 2
        for i in range(nslices):
            for h in range(2):
                if dq_state["hook"] is not None:
                    dq_state["hook"]()
                sl = (2 * i + h) % 2
                plan.wait("scalar", sV, use_v[sl])
                lv = dma(plan, "scalar", dsDqi[sl], dqi[sl][:, 0:w2],
                         qwT[i * 128:(i + 1) * 128, h * w2:(h + 1) * w2])
                plan.wait("vector", dsDqi[sl], lv)
                plan.wait("vector", dsDqo[sl], use_st[sl])
                for g in range(g2):
                    gg = h * g2 + g
                    fn = lambda e, sl=sl, g=g, i=i, gg=gg: e.tensor_scalar(
                        dqo[sl][:, g * 128:(g + 1) * 128],
                        dqi[sl][:, g * 128:(g + 1) * 128],
                        zT_s[:, i * ngroups + gg:i * ngroups + gg + 1],
                        sT_s[:, i * ngroups + gg:i * ngroups + gg + 1],
                        ALU.subtract, ALU.mult)
                    if g == g2 - 1:
                        use_v[sl] = op(plan, "vector", sV, fn)
                    else:
                        opq(plan, "vector", fn)
                plan.wait("gpsimd", sV, use_v[sl])
                use_st[sl] = dma(plan, "gpsimd", dsDqo[sl],
                                 dT_out[i * 128:(i + 1) * 128, h * w2:(h + 1) * w2],
                                 dqo[sl][:, 0:w2])

    with sbL("awl0", [128, FF], BF16) as awl0, \
         sbL("awl1", [128, FF], BF16) as awl1:
        awl = [awl0, awl1]
        dsAwl = [plan.dsem("sync", f"dsAwl{k}") for k in range(2)]
        dsAwSt = [plan.dsem("gpsimd", f"dsAwSt{k}") for k in range(2)]
        aw_st = [0, 0]
        aw_i = [0]
        aw_items = []
        for src16, srcrows, dst, ncols, width in (
                (wkvc16, 640, wkvcTh, NCD, 640),
                (wqc16, QL, wqcTh, NCD, QL),
                (wqup16, NH * QH, wqupTh, 12, NH * QH),
                (wkvup16, NH * 256, wkvupTh, 4, NH * 256)):
            for c in range(ncols):
                aw_items.append((src16, srcrows, dst, c, width))

        def aw_emit_one():
            if not aw_items:
                return
            src16, srcrows, dst, c, width = aw_items.pop(0)
            sl = aw_i[0] % 2
            aw_i[0] += 1
            if aw_i[0] == 1:
                plan.wait("sync", dsCast["a"], cast_done["a"])
            plan.wait("sync", dsAwSt[sl], aw_st[sl])
            lv = dma(plan, "sync", dsAwl[sl], awl[sl][:, 0:width],
                     src16[0:srcrows, c * 128:(c + 1) * 128], transpose=True)
            plan.wait("gpsimd", dsAwl[sl], lv)
            aw_st[sl] = dma(plan, "gpsimd", dsAwSt[sl],
                            dst[:, c * width:(c + 1) * width], awl[sl][:, 0:width])

        dq_state["hook"] = aw_emit_one
        dequant(qw_g, dTg, zgT, sgT, 86, D, 32)
        dequant(qw_u, dTu, zuT, suT, 86, D, 32)
        dequant(qw_d, dTd, zdT, sdT, 32, FF, 86)
        while aw_items:
            aw_emit_one()

    barrier(plan, sB, compute=CMP)
    dq_es.close()
    zs_es.close()

    rl_es = ExitStack()
    rl = [rl_es.enter_context(sbL(f"rl{k}", [128, 2 * FF], BF16)) for k in range(2)]
    RLQ = ("sync", "sync")
    dsRl = [plan.dsem(RLQ[k], f"dsRl{k}") for k in range(2)]
    dsRlw = [plan.dsem("gpsimd", f"dsRlw{k}") for k in range(2)]
    rl_state = {"use_w": [0, 0]}

    def relayout(dT_in, wB_out, ngrp, nblk, width):
        use_w = rl_state["use_w"]
        last = 0
        for grp in range(ngrp):
            sl = grp % 2
            plan.wait(RLQ[sl], dsRlw[sl], use_w[sl])
            lv = 0
            for k in range(2):
                c = grp * 2 + k
                lv = dma(plan, RLQ[sl], dsRl[sl], rl[sl][:, k * width:k * width + width],
                         dT_in[:, c * 128:(c + 1) * 128], transpose=True)
            plan.wait("gpsimd", dsRl[sl], lv)
            for blk in range(nblk):
                src = rl[sl][:, 0:2 * width].rearrange("p (k f) -> p k f", k=2)[
                    :, :, blk * 256:(blk + 1) * 256]
                dst = wB_out[blk, grp * 256:(grp + 1) * 256, :].rearrange(
                    "(k p) j -> p k j", k=2)
                last = dma(plan, "gpsimd", dsRlw[sl], dst, src)
            use_w[sl] = last
        return last

    relayout(dTg, wgB, 16, 43, FF)
    relayout(dTu, wuB, 16, 43, FF)
    relayout(dTd, wdB, 43, 16, D)

    barrier(plan, sB, compute=CMP)
    rl_es.close()

    dsXnb = [plan.dsem("scalar", f"dsXnb{k}") for k in range(3)]

    def emit_xn(x_src, mb, invb, xn_dst):
        with sbL("xnb0", [128, T], F32) as xnb0, \
             sbL("xnb1", [128, T], F32) as xnb1:
            xnb = [xnb0, xnb1]
            use_v = [0, 0]
            last = 0
            for c in range(NCD):
                sl = c % 2
                plan.wait("scalar", sV, use_v[sl])
                lv = dma(plan, "scalar", dsXnb[sl], xnb[sl][:], x_src[c * 128:(c + 1) * 128, :])
                plan.wait("vector", dsXnb[sl], lv)
                sv_ = op(plan, "vector", sV, lambda e, sl=sl: e.tensor_sub(xnb[sl][:], xnb[sl][:], mb[:]))
                plan.wait("vector", sV, sv_)
                last = use_v[sl] = op(plan, "vector", sV, lambda e, sl=sl, c=c: e.tensor_mul(
                    xn_dst[:, c * T:(c + 1) * T], xnb[sl][:], invb[:]))
        return last

    dsWs = [plan.dsem("sync", f"dsWs{k}") for k in range(3)]
    dsHt = [plan.dsem("gpsimd", f"dsHt{k}") for k in range(2)]

    def emit_gateup(xn_src, wgB_in, wuB_in, hact_out):
        with sbL("ws0", [128, 32 * 256], BF16) as ws0, \
             sbL("ws1", [128, 32 * 256], BF16) as ws1, \
             sbL("ws2", [128, 32 * 256], BF16) as ws2, \
             sbL("gu_ga", [128, T], BF16) as ga, \
             sbL("gu_ua", [128, T], BF16) as ua, \
             sbL("gu_ht0", [128, T], BF16) as ht0, \
             sbL("gu_ht1", [128, T], BF16) as ht1:
            ws = [ws0, ws1, ws2]
            use_pe = [0, 0, 0]
            ldv = [0, 0, 0]
            htb = [ht0, ht1]
            use_ht = [0, 0]
            silu_done = 0
            ucopy_done = 0
            mul_done = 0
            st_done = 0
            for hp in range(43):
                sg_ = (2 * hp) % 3
                su_ = (2 * hp + 1) % 3
                plan.wait("sync", sT, use_pe[sg_])
                ldv[sg_] = dma(plan, "sync", dsWs[sg_],
                               ws[sg_][:].rearrange("p (c j) -> p c j", j=256),
                               wgB_in[hp].rearrange("(c p) j -> p c j", p=128))
                plan.wait("sync", sT, use_pe[su_])
                ldv[su_] = dma(plan, "sync", dsWs[su_],
                               ws[su_][:].rearrange("p (c j) -> p c j", j=256),
                               wuB_in[hp].rearrange("(c p) j -> p c j", p=128))
                for sub in range(2):
                    ht = hp * 2 + sub
                    plan.wait("tensor", dsWs[sg_], ldv[sg_])
                    plan.wait("tensor", sS, silu_done)
                    for c in range(NCD):
                        for nt in range(4):
                            fn = lambda e, c=c, nt=nt, sg_=sg_, sub=sub: e.matmul(
                                pb[nt][:, :],
                                ws[sg_][:, c * 256 + sub * 128:c * 256 + (sub + 1) * 128],
                                xn_src[:, c * T + nt * 512:c * T + (nt + 1) * 512],
                                start=(c == 0), stop=(c == NCD - 1))
                            if c == NCD - 1 and nt == 3:
                                gmm = op(plan, "tensor", sT, fn)
                            else:
                                opq(plan, "tensor", fn)
                    plan.wait("tensor", dsWs[su_], ldv[su_])
                    plan.wait("tensor", sV, ucopy_done)
                    for c in range(NCD):
                        for nt in range(4):
                            fn = lambda e, c=c, nt=nt, su_=su_, sub=sub: e.matmul(
                                pb[4 + nt][:, :],
                                ws[su_][:, c * 256 + sub * 128:c * 256 + (sub + 1) * 128],
                                xn_src[:, c * T + nt * 512:c * T + (nt + 1) * 512],
                                start=(c == 0), stop=(c == NCD - 1))
                            if c == NCD - 1 and nt == 3:
                                umm = op(plan, "tensor", sT, fn)
                            else:
                                opq(plan, "tensor", fn)
                    if sub == 1:
                        use_pe[sg_] = use_pe[su_] = umm
                    plan.wait("scalar", sT, gmm)
                    plan.wait("scalar", sV, mul_done)
                    for nt in range(4):
                        fn = lambda e, nt=nt: e.activation(
                            ga[:, nt * 512:(nt + 1) * 512], pb[nt][:, :], AF.Silu)
                        if nt == 3:
                            silu_done = op(plan, "scalar", sS, fn)
                        else:
                            opq(plan, "scalar", fn)
                    plan.wait("vector", sT, umm)
                    plan.wait("vector", sV, mul_done)
                    for nt in range(4):
                        fn = lambda e, nt=nt: e.tensor_copy(
                            ua[:, nt * 512:(nt + 1) * 512], pb[4 + nt][:, :])
                        if nt == 3:
                            ucopy_done = op(plan, "vector", sV, fn)
                        else:
                            opq(plan, "vector", fn)
                    hsl = ht % 2
                    plan.wait("vector", sS, silu_done)
                    plan.wait("vector", sV, ucopy_done)
                    plan.wait("vector", dsHt[hsl], use_ht[hsl])
                    mul_done = op(plan, "vector", sV, lambda e, hsl=hsl: e.tensor_mul(
                        htb[hsl][:], ga[:], ua[:]))
                    plan.wait("gpsimd", sV, mul_done)
                    use_ht[hsl] = st_done = dma(plan, "gpsimd", dsHt[hsl],
                                                hact_out[ht * 128:(ht + 1) * 128, :],
                                                htb[hsl][:])
        return st_done

    dsHq = plan.dsem("sync", "dsHq")
    dsWd = [plan.dsem("sync", f"dsWd{k}") for k in range(2)]
    dsXr = [plan.dsem("scalar", f"dsXr{k}") for k in range(2)]
    dsX1 = [plan.dsem("gpsimd", f"dsX1{k}") for k in range(2)]

    def emit_down(hact_in, wB_in, xres_src, xout_dst, half_scale):
        with sbL("dn_hq", [128, 86 * 512], BF16) as hq, \
             sbL("dn_wd0", [128, 86 * 256], BF16) as wd0, \
             sbL("dn_wd1", [128, 86 * 256], BF16) as wd1, \
             sbL("dn_xrb0", [128, 512], F32) as xrb0, \
             sbL("dn_xrb1", [128, 512], F32) as xrb1, \
             sbL("dn_x1b0", [128, 512], F32) as x1b0, \
             sbL("dn_x1b1", [128, 512], F32) as x1b1:
            wd = [wd0, wd1]
            xrb = [xrb0, xrb1]
            x1b = [x1b0, x1b1]
            use_wd = [0, 0]
            use_x1 = [0, 0]
            hq_use = 0
            st_done = 0
            evac_done = 0
            for tq in range(4):
                plan.wait("sync", sT, hq_use)
                lv = 0
                for part in range(8):
                    h0 = part * 11
                    h1 = min(86, h0 + 11)
                    if h0 >= 86:
                        break
                    src = hact_in.rearrange("(hc p) t -> p hc t", p=128)[
                        :, h0:h1, tq * 512:(tq + 1) * 512]
                    dstv = hq[:, h0 * 512:h1 * 512].rearrange("p (hc j) -> p hc j", j=512)
                    lv = dma(plan, "sync", dsHq, dstv, src)
                hq_v = lv
                for dp in range(16):
                    sl = dp % 2
                    plan.wait("sync", sT, use_wd[sl])
                    wv = dma(plan, "sync", dsWd[sl],
                             wd[sl][:].rearrange("p (hc j) -> p hc j", j=256),
                             wB_in[dp].rearrange("(hc p) j -> p hc j", p=128))
                    plan.wait("tensor", dsWd[sl], wv)
                    plan.wait("tensor", dsHq, hq_v)
                    plan.wait("tensor", sV, evac_done)
                    for hc in range(86):
                        for k in range(2):
                            fn = lambda e, hc=hc, k=k, sl=sl: e.matmul(
                                pb[k][:, :],
                                wd[sl][:, hc * 256 + k * 128:hc * 256 + (k + 1) * 128],
                                hq[:, hc * 512:(hc + 1) * 512],
                                start=(hc == 0), stop=(hc == 85))
                            if hc == 85 and k == 1:
                                use_wd[sl] = mmv = op(plan, "tensor", sT, fn)
                            else:
                                opq(plan, "tensor", fn)
                    if dp == 15:
                        hq_use = mmv
                    for k in range(2):
                        dt = dp * 2 + k
                        xsl = dt % 2
                        plan.wait("scalar", dsX1[xsl], use_x1[xsl])
                        xv = dma(plan, "scalar", dsXr[xsl], xrb[xsl][:],
                                 xres_src[dt * 128:(dt + 1) * 128, tq * 512:(tq + 1) * 512])
                        plan.wait("vector", dsXr[xsl], xv)
                        plan.wait("vector", sT, mmv)
                        ev = op(plan, "vector", sV, lambda e, k=k, xsl=xsl, hs=half_scale:
                                e.scalar_tensor_tensor(x1b[xsl][:], pb[k][:, :], hs,
                                                       xrb[xsl][:], ALU.mult, ALU.add))
                        if k == 1:
                            evac_done = ev
                        plan.wait("gpsimd", sV, ev)
                        use_x1[xsl] = st_done = dma(
                            plan, "gpsimd", dsX1[xsl],
                            xout_dst[dt * 128:(dt + 1) * 128, tq * 512:(tq + 1) * 512],
                            x1b[xsl][:])
            return st_done

    for xsrc, x1dst in ((xoT, x1oT), (xpT, x1pT)):
        ln1_es = ExitStack()
        mb1 = ln1_es.enter_context(sbR("mb1", [128, T], F32))
        invb1 = ln1_es.enter_context(sbR("invb1", [128, T], F32))
        emit_stats(xsrc, mb1, invb1)
        barrier(plan, sB, compute=CMP)

        xn_es = ExitStack()
        xn = xn_es.enter_context(sbL("xn", [128, NCD * T], BF16))
        xn_done = emit_xn(xsrc, mb1, invb1, xn)
        ln1_es.close()

        plan.wait("tensor", sV, xn_done)
        plan.wait("sync", sV, xn_done)
        emit_gateup(xn, wgB, wuB, hact)
        barrier(plan, sB, compute=CMP)
        xn_es.close()

        emit_down(hact, wdB, xsrc, x1dst, 0.5)
        barrier(plan, sB, compute=CMP)

    dsKvT = plan.dsem("sync", "dsKvT")
    dsKvSt = plan.dsem("gpsimd", "dsKvSt")
    dsXn2St = plan.dsem("gpsimd", "dsXn2St")
    kv_st = 0
    for src, coff, store_xn in ((x1oT, 0, True), (x1pT, T, False)):
        ln2_es = ExitStack()
        mb2 = ln2_es.enter_context(sbR("mb2", [128, T], F32))
        invb2 = ln2_es.enter_context(sbR("invb2", [128, T], F32))
        emit_stats(src, mb2, invb2)
        barrier(plan, sB, compute=CMP)

        xn2_es = ExitStack()
        xn2 = xn2_es.enter_context(sbL("xn2", [128, NCD * T], BF16))
        xn2_done = emit_xn(src, mb2, invb2, xn2)
        ln2_es.close()

        if store_xn:
            plan.wait("gpsimd", sV, xn2_done)
            for c in range(NCD):
                dma(plan, "gpsimd", dsXn2St, xn2h[c * 128:(c + 1) * 128, :],
                    xn2[:, c * T:(c + 1) * T])

        wkvcT = xn2_es.enter_context(sbL("wkvcT", [128, NCD * 640], BF16))
        plan.wait("sync", sV, xn2_done)
        kvt_v = dma(plan, "sync", dsKvT, wkvcT[:], wkvcTh)
        plan.wait("tensor", dsKvT, kvt_v)
        plan.wait("tensor", sV, xn2_done)
        with sbL("kvb", [128, 512], BF16) as kvb:
            ev_bank = [0] * 5
            for tq in range(4):
                for ft in range(5):
                    fw = 128 if ft < 4 else 64
                    plan.wait("tensor", sV, ev_bank[ft])
                    for c in range(NCD):
                        fn = lambda e, c=c, ft=ft, fw=fw, tq=tq: e.matmul(
                            pb[ft][0:fw, :],
                            wkvcT[:, c * 640 + ft * 128:c * 640 + ft * 128 + fw],
                            xn2[:, c * T + tq * 512:c * T + (tq + 1) * 512],
                            start=(c == 0), stop=(c == NCD - 1))
                        if c == NCD - 1:
                            mmv = op(plan, "tensor", sT, fn)
                        else:
                            opq(plan, "tensor", fn)
                    plan.wait("vector", sT, mmv)
                    plan.wait("vector", dsKvSt, kv_st)
                    ev_bank[ft] = op(plan, "vector", sV, lambda e, ft=ft, fw=fw:
                                     e.tensor_scalar_add(kvb[0:fw, :], pb[ft][0:fw, :],
                                                         bkvcT_s[0:fw, ft:ft + 1]))
                    plan.wait("gpsimd", sV, ev_bank[ft])
                    kv_st = dma(plan, "gpsimd", dsKvSt,
                                kv16[ft * 128:ft * 128 + fw,
                                     coff + tq * 512:coff + (tq + 1) * 512],
                                kvb[0:fw, :])
        barrier(plan, sB, compute=CMP)
        xn2_es.close()

    emit_l2_attention(
        nc, plan, es, sbL, pb, sV, sS, sT, sB, CMP, id_s, ones_s,
        bqcT_s, bqupT_s, bkvupT_s, boT_s,
        x1oT, x2T, kv16, wqcTh, wqupTh, wkvupTh, wo16, xn2h,
        plan.dsem, dsConst, v_const,
        dsCast, cast_done, dma, barrier)

    ln3_es = ExitStack()
    mb3 = ln3_es.enter_context(sbR("mb3", [128, T], F32))
    invb3 = ln3_es.enter_context(sbR("invb3", [128, T], F32))
    emit_stats(x2T, mb3, invb3)
    barrier(plan, sB, compute=CMP)

    xn3_es = ExitStack()
    xn3 = xn3_es.enter_context(sbL("xn3", [128, NCD * T], BF16))
    xn3_done = emit_xn(x2T, mb3, invb3, xn3)
    ln3_es.close()
    plan.wait("tensor", sV, xn3_done)
    plan.wait("sync", sV, xn3_done)
    emit_gateup(xn3, wgB, wuB, hact)
    barrier(plan, sB, compute=CMP)
    xn3_es.close()
    emit_down(hact, wdB, x2T, outT, 0.5)

    for eng, c in plan.dsems:
        plan.wait(eng, c, c.v)

    with nc.Block() as block:
        plan.run(block)
    es.close()
    return nc

def _pad_bias(b, rows, cols=128):
    out = np.zeros((rows, cols), dtype=np.float32)
    flat = np.asarray(b, dtype=np.float32).reshape(-1)
    out.reshape(-1)[: flat.size] = flat
    return out


def _common_maps(inputs):
    return {
        "qw_gateT": np.ascontiguousarray(np.asarray(inputs["L0_ffn_qw_gate"]).T),
        "z_gate": np.asarray(inputs["L0_ffn_z_gate"]),
        "s_gate": np.asarray(inputs["L0_ffn_s_gate"]),
        "qw_upT": np.ascontiguousarray(np.asarray(inputs["L0_ffn_qw_up"]).T),
        "z_up": np.asarray(inputs["L0_ffn_z_up"]),
        "s_up": np.asarray(inputs["L0_ffn_s_up"]),
        "qw_downT": np.ascontiguousarray(np.asarray(inputs["L0_ffn_qw_down"]).T),
        "z_down": np.asarray(inputs["L0_ffn_z_down"]),
        "s_down": np.asarray(inputs["L0_ffn_s_down"]),
        "id_f": np.eye(128, dtype=np.float32),
        "ones_b": np.ones((128, 128), dtype=ml_dtypes.bfloat16),
        "epsb": np.full((128, 1), EPS, dtype=np.float32),
    }


def fused_in_maps(inputs):
    x = np.asarray(inputs["L0_attn_hidden_size"])
    common = _common_maps(inputs)
    common["Wkv_c"] = np.concatenate(
        [np.asarray(inputs["L0_attn_Wkv_c"], dtype=np.float32),
         np.zeros((64, D), np.float32)], axis=0)
    common["bkvc_t"] = np.ascontiguousarray(_pad_bias(inputs["L0_attn_bkv_c"], 5).T)
    common["Wq_c"] = np.asarray(inputs["L0_attn_Wq_c"], dtype=np.float32)
    common["bqc_t"] = np.ascontiguousarray(_pad_bias(inputs["L0_attn_bq_c"], 12).T)
    common["Wq_up"] = np.asarray(inputs["L0_attn_Wq_up"], dtype=np.float32)
    bqup = np.asarray(inputs["L0_attn_bq_up"], dtype=np.float32)
    bq_t = np.zeros((128, 64), np.float32)
    for h in range(NH):
        bq_t[:, 2 * h] = bqup[h * QH:h * QH + NOPE]
        bq_t[0:ROPE, 2 * h + 1] = bqup[h * QH + NOPE:(h + 1) * QH]
    common["bqup_t"] = bq_t
    common["Wkv_up"] = np.asarray(inputs["L0_attn_Wkv_up"], dtype=np.float32)
    common["bkvup_t"] = np.ascontiguousarray(_pad_bias(inputs["L0_attn_bkv_up"], 64).T)
    common["Wo"] = np.asarray(inputs["L0_attn_Wo"], dtype=np.float32)
    common["bo_t"] = np.ascontiguousarray(_pad_bias(inputs["L0_attn_bo"], 32).T)
    maps = []
    for c in range(8):
        b, half = divmod(c, 2)
        m = dict(common)
        m["xoT"] = np.ascontiguousarray(x[b, half * T:(half + 1) * T, :].T)
        m["xpT"] = np.ascontiguousarray(x[b, (1 - half) * T:(2 - half) * T, :].T)
        maps.append(m)
    return maps


def kernel(**inputs):
    maps = fused_in_maps(inputs)
    nc = build_fused()
    res = run_bass_kernel_spmd(nc, maps, core_ids=list(range(8)))
    out = np.empty((B, S, D), dtype=np.float32)
    for c in range(8):
        b, half = divmod(c, 2)
        out[b, half * T:(half + 1) * T, :] = np.asarray(res.results[c]["outT"]).T
    return out
